# version 19
# baseline (speedup 1.0000x reference)
import numpy as np
import concourse.bass as bass
import concourse.mybir as mybir
from concourse.bass_utils import run_bass_kernel_spmd
from concourse.alu_op_type import AluOpType as OP
from contextlib import ExitStack

F32 = mybir.dt.float32
BF16 = mybir.dt.bfloat16
I32 = mybir.dt.int32
AF = mybir.ActivationFunctionType
AX = mybir.AxisListType

B, T, C = 2, 8192, 2048
NH, HD, HG = 16, 128, 4
FPC = HG * HD
NTOK = 2048
RSQ = 1.0 / np.sqrt(128.0)
NEG = -1e30
EPS = 1e-5
INTER = 8192
MOE = 1024
ENGS = ("tensor", "vector", "scalar", "gpsimd", "sync")
_DEBUG_TAPS = False


class Plan:
    def __init__(self):
        self.ops = {e: [] for e in ENGS}
        self.cnt = {}
        self.sched = {}
        self.all_sched = {}

    def check_deadlock(self, tag):
        pos = {e: 0 for e in self.sched}
        sem = dict(self.final_sem) if hasattr(self, "final_sem") else {}
        progress = True
        while progress:
            progress = False
            for e, ops in self.sched.items():
                while pos[e] < len(ops):
                    wl, inc = ops[pos[e]]
                    if all(sem.get(s, 0) >= v for s, v in wl):
                        if inc:
                            sem[inc[0]] = sem.get(inc[0], 0) + inc[1]
                        pos[e] += 1
                        progress = True
                    else:
                        break
        stuck = {e: (pos[e], len(ops)) for e, ops in self.sched.items() if pos[e] < len(ops)}
        if stuck:
            print(f"DEADLOCK in {tag}:")
            for e, (p, n) in stuck.items():
                wl, inc = self.sched[e][p]
                unmet = [(s, v, sem.get(s, 0)) for s, v in wl if sem.get(s, 0) < v]
                print(f"  {e}: stuck at op {p}/{n}, unmet={unmet}")
            raise RuntimeError(f"deadlock in {tag}")
        self.final_sem = sem
        self.sched = {}

    def val(self, sem):
        return self.cnt.get(sem, 0)

    def bump(self, sem, k):
        self.cnt[sem] = self.cnt.get(sem, 0) + k
        return self.cnt[sem]

    def emit(self, eng, fn, waits=(), inc=None):
        wl = [(s, v) for (s, v) in waits if v > 0]
        if inc is not None:
            self.bump(*inc)
        self.sched.setdefault(eng, []).append((wl, inc))

        def op(e, sems):
            for s, v in wl:
                e.wait_ge(sems[s], v)
            instr = fn(e, sems)
            if inc is not None:
                assert instr is not None
                instr.then_inc(sems[inc[0]], inc[1])
        self.ops[eng].append(op)

    def replay(self, block, sems):
        for eng in ENGS:
            ops = self.ops[eng]
            if not ops:
                continue

            def body(e, ops=ops):
                for op in ops:
                    op(e, sems)
            getattr(block, eng)(body)
        self.ops = {e: [] for e in ENGS}


SEM_NAMES = [
    "dS", "dG", "dG0", "dG1", "cc", "mi", "dInit", "dWq", "dMask", "dF2", "dYr",
    "dX0", "dX1", "dX2", "dX3", "dX4", "dX5", "dX6", "dX7", "dQb0", "dQb1", "dAg0", "dAg1",
    "dY0", "dY1", "dWp", "dOg0", "dOg1", "dWg0", "dWg1", "dDt0", "dDt1", "dDt2", "dDt3",
    "dOut0", "dOut1", "dC0", "dC1", "dC2", "dYT0", "dYT1", "dYS0", "dYS1",
    "sX", "sW", "sQK", "sQKe", "sTb", "sVe", "sXr",
    "sP", "sPd1", "sPd2", "sTr", "sPTe", "sAVd", "sNegm", "sRl", "sTrl",
    "sRlrow", "sBc", "sOTe", "sOTf", "sMask", "sLd",
    "sPP", "sPPe", "sSq", "sRstd", "sNm", "sV2", "sXo",
    "sYc", "sYt", "sYe", "sRt", "sY1",
    "sMMa", "sSg", "sGu", "sGtp", "sGte", "sDm", "sDe", "sT2", "sOut", "sY2e",
]


def build_nc():
    nc = bass.Bass("TRN2", target_bir_lowering=False)
    xT = nc.dram_tensor("xT", [C, T], F32, kind="ExternalInput").ap()
    x_own = nc.dram_tensor("x_own", [NTOK, C], F32, kind="ExternalInput").ap()
    wqkv = nc.dram_tensor("wqkv", [C, 3 * FPC], F32, kind="ExternalInput").ap()
    bqk = nc.dram_tensor("bqk", [128, 8], F32, kind="ExternalInput").ap()
    bv_row = nc.dram_tensor("bv_row", [1, FPC], F32, kind="ExternalInput").ap()
    wproj = nc.dram_tensor("wproj", [C, C], F32, kind="ExternalInput").ap()
    bp_row = nc.dram_tensor("bp_row", [1, C], F32, kind="ExternalInput").ap()
    rwT = nc.dram_tensor("rwT", [C, 8], F32, kind="ExternalInput").ap()
    wgT = nc.dram_tensor("wgT", [C, INTER], F32, kind="ExternalInput").ap()
    wuT = nc.dram_tensor("wuT", [C, INTER], F32, kind="ExternalInput").ap()
    wdT = nc.dram_tensor("wdT", [INTER, C], F32, kind="ExternalInput").ap()
    rgT = nc.dram_tensor("rgT", [C, MOE], F32, kind="ExternalInput").ap()
    ruT = nc.dram_tensor("ruT", [C, MOE], F32, kind="ExternalInput").ap()
    rdT = nc.dram_tensor("rdT", [MOE, C], F32, kind="ExternalInput").ap()
    masks = nc.dram_tensor("masks", [128, 2048], F32, kind="ExternalInput").ap()
    out = nc.dram_tensor("out", [NTOK, C], F32, kind="ExternalOutput").ap()
    KD = "ExternalOutput" if _DEBUG_TAPS else "Internal"
    q_scr = nc.dram_tensor("q_scr", [FPC, T], BF16, kind=KD).ap()
    ag_in = nc.dram_tensor("ag_in", [8 * FPC, T // 8], BF16).ap()
    ag_tap = nc.dram_tensor("ag_tap", [8 * FPC, T // 8], BF16, kind=KD).ap() if _DEBUG_TAPS else None
    ag_out = nc.dram_tensor("ag_out", [8 * C, T // 8], BF16).ap()
    y1_scr = nc.dram_tensor("y1_scr", [NTOK, C], F32, kind=KD).ap()
    y1T_scr = nc.dram_tensor("y1T_scr", [C, NTOK], BF16).ap()
    wg_bf = nc.dram_tensor("wg_bf", [C, INTER], BF16, kind=KD).ap()
    wu_bf = nc.dram_tensor("wu_bf", [C, INTER], BF16).ap()
    wd_bf = nc.dram_tensor("wd_bf", [INTER, C], BF16).ap()
    wp_bf = nc.dram_tensor("wp_bf", [C, C], BF16).ap()
    rg_bf = nc.dram_tensor("rg_bf", [C, MOE], BF16).ap()
    ru_bf = nc.dram_tensor("ru_bf", [C, MOE], BF16).ap()
    rd_bf = nc.dram_tensor("rd_bf", [MOE, C], BF16).ap()

    g = ExitStack()
    sems = {s: g.enter_context(nc.semaphore(s)) for s in SEM_NAMES}
    ident_bf = g.enter_context(nc.sbuf_tensor("ident_bf", [128, 128], BF16))
    ones32 = g.enter_context(nc.sbuf_tensor("ones32", [1, 128], F32))
    iot = g.enter_context(nc.sbuf_tensor("iot", [128, 128], I32))
    bqk_sb = g.enter_context(nc.sbuf_tensor("bqk_sb", [128, 8], F32))
    bvb = g.enter_context(nc.sbuf_tensor("bvb", [128, FPC], F32))
    comb_sb = g.enter_context(nc.sbuf_tensor("comb_sb", [128, 16], F32))
    cstab_sb = g.enter_context(nc.sbuf_tensor("cstab_sb", [128, 1], F32))
    ones_bf = g.enter_context(nc.sbuf_tensor("ones_bf", [128, 1], BF16))

    P = Plan()

    s1 = ExitStack()
    K_sb = s1.enter_context(nc.sbuf_tensor("K_sb", [128, HG * T], BF16))
    V_sb = s1.enter_context(nc.sbuf_tensor("V_sb", [128, (T // 128) * FPC], BF16))
    s1a = ExitStack()
    wqkv_sb = s1a.enter_context(nc.sbuf_tensor("wqkv_sb", [128, 16 * 3 * FPC], BF16))
    xbf = s1a.enter_context(nc.sbuf_tensor("xbf", [128, 8192], BF16))
    xstage = s1a.enter_context(nc.sbuf_tensor("xstage", [128, 5 * 512], F32))
    qbuf = s1a.enter_context(nc.sbuf_tensor("qbuf", [128, 2 * 512], BF16))
    ps1 = ExitStack()
    pqk = [ps1.enter_context(nc.psum_tensor(f"pqk{i}", [128, 512], F32)) for i in range(8)]

    P.emit("sync", lambda e, s: e.dma_start(out=bqk_sb[:], in_=bqk), inc=("dInit", 16))
    P.emit("sync", lambda e, s: e.dma_start(out=xstage[0:1, 2048:2560], in_=bv_row), inc=("dInit", 16))
    P.emit("gpsimd", lambda e, s: e.iota(iot[:], pattern=[[1, 128]], base=0,
                                         channel_multiplier=-1), inc=("mi", 1))
    P.emit("vector", lambda e, s: e.memset(ones32[:], 1.0))
    P.emit("vector", lambda e, s: e.memset(cstab_sb[:], -14.0))
    P.emit("vector", lambda e, s: e.memset(ones_bf[:], 1.0))
    P.emit("vector", lambda e, s: e.tensor_scalar(ident_bf[:], iot[:], 0.0, None, OP.is_equal),
           waits=[("mi", 1)])
    P.emit("vector", lambda e, s: e.drain(), inc=("mi", 1))
    P.emit("tensor", lambda e, s: e.matmul(pqk[0][:], ones32[0:1, :], xstage[0:1, 2048:2560],
                                           start=True, stop=True),
           waits=[("mi", 2), ("dInit", 32)], inc=("mi", 1))
    P.emit("vector", lambda e, s: e.tensor_copy(bvb[:, 0:512], pqk[0][:]),
           waits=[("mi", 3)])
    P.emit("vector", lambda e, s: e.drain(), inc=("mi", 1))

    for hq in range(32):
        hc, half = divmod(hq, 2)
        P.emit("sync", (lambda hc, half: lambda e, s: e.dma_start(
            out=xstage[:, 0:768], in_=wqkv[hc * 128:(hc + 1) * 128,
                                           half * 768:(half + 1) * 768]))(hc, half),
            waits=[("sW", P.val("sW")), ("mi", 4) if hq == 0 else ("mi", 0)], inc=("dWq", 16))
        P.emit("vector", (lambda hc, half: lambda e, s: e.tensor_copy(
            wqkv_sb[:, hc * 1536 + half * 768: hc * 1536 + (half + 1) * 768],
            xstage[:, 0:768]))(hc, half),
            waits=[("dWq", P.val("dWq"))], inc=("sW", 1))

    qdma = []
    for tb in range(16):
        xoff = 0
        for hc in range(16):
            slot = hc % 5
            n = tb * 16 + hc
            P.emit("sync", (lambda hc, slot, tb: lambda e, s: e.dma_start(
                out=xstage[:, slot * 512:(slot + 1) * 512],
                in_=xT[hc * 128:(hc + 1) * 128, tb * 512:(tb + 1) * 512]))(hc, slot, tb),
                waits=[("sX", n - 4)] + ([("sW", 32)] if n < 5 else [])
                      + ([("mi", 4)] if n == 4 else []),
                inc=(f"dX{slot}", 16))
            w = [(f"dX{slot}", P.val(f"dX{slot}"))]
            if tb >= 1:
                w.append(("sXr", 16 * (tb - 1) + hc + 1))
            P.emit("vector", (lambda hc, slot, xoff: lambda e, s: e.tensor_copy(
                xbf[:, xoff + hc * 512: xoff + (hc + 1) * 512],
                xstage[:, slot * 512:(slot + 1) * 512]))(hc, slot, xoff),
                waits=w, inc=("sX", 1))
        for hc in range(16):
            for ft in range(8):
                col0 = (ft % 4) * 128 + (ft // 4) * 512
                w = []
                if ft == 0:
                    w.append(("sX", 16 * tb + hc + 1))
                if hc == 0:
                    if ft == 0:
                        w.append(("mi", 4))
                        w.append(("sW", 32))
                    if tb >= 1:
                        w.append(("sQKe", 8 * (tb - 1) + ft + 1))
                        if ft < 4:
                            w.append(("sVe", 4 * (tb - 1) + ft + 1))
                P.emit("tensor", (lambda hc, ft, col0, tb: lambda e, s: e.matmul(
                    pqk[ft][:], wqkv_sb[:, hc * 1536 + col0: hc * 1536 + col0 + 128],
                    xbf[:, hc * 512:(hc + 1) * 512],
                    start=(hc == 0), stop=(hc == 15)))(hc, ft, col0, tb),
                    waits=w, inc=("sQK", 1) if hc == 15 else None)
        for ft in range(8):
            gq = P.val("sQK") - 7 + ft
            if ft < 4:
                qb = (tb * 4 + ft) % 2
                wts = [("sQK", gq), ("dInit", 32)]
                if len(qdma) >= 2:
                    wts.append((f"dQb{qb}", qdma[-2][1]))
                P.emit("vector", (lambda ft, qb: lambda e, s: e.tensor_scalar(
                    qbuf[:, qb * 512:(qb + 1) * 512], pqk[ft][:],
                    bqk_sb[:, ft:ft + 1], RSQ, OP.add, OP.mult))(ft, qb),
                    waits=wts, inc=("sQKe", 1))
                P.emit("sync", (lambda ft, qb, tb: lambda e, s: e.dma_start(
                    out=q_scr[ft * 128:(ft + 1) * 128, tb * 512:(tb + 1) * 512],
                    in_=qbuf[:, qb * 512:(qb + 1) * 512]))(ft, qb, tb),
                    waits=[("sQKe", P.val("sQKe"))], inc=(f"dQb{qb}", 16))
                qdma.append((qb, P.val(f"dQb{qb}")))
            else:
                P.emit("vector", (lambda ft, tb: lambda e, s: e.tensor_scalar(
                    K_sb[:, (ft - 4) * T + tb * 512: (ft - 4) * T + (tb + 1) * 512],
                    pqk[ft][:], bqk_sb[:, ft:ft + 1], None, OP.add))(ft, tb),
                    waits=[("sQK", gq), ("dInit", 32)], inc=("sQKe", 1))
        for hc in range(16):
            for tt in range(4):
                w = []
                if hc == 0:
                    w.append(("sQKe", 8 * tb + tt + 1))
                P.emit("tensor", (lambda hc, tt, tb: lambda e, s: e.matmul(
                    pqk[tt][:], xbf[:, hc * 512 + tt * 128: hc * 512 + tt * 128 + 128],
                    wqkv_sb[:, hc * 1536 + 1024: hc * 1536 + 1536],
                    start=(hc == 0), stop=(hc == 15)))(hc, tt, tb),
                    waits=w, inc=("sXr", 1) if tt == 3 else None)
        for tt in range(4):
            P.emit("vector", (lambda tt, tb: lambda e, s: e.tensor_tensor(
                V_sb[:, (tb * 4 + tt) * FPC:(tb * 4 + tt + 1) * FPC], pqk[tt][:],
                bvb[:, 0:512], OP.add))(tt, tb),
                waits=[("sXr", 16 * (tb + 1)), ("mi", 4)], inc=("sVe", 1))
    P.emit("sync", lambda e, s: e.nop(),
           waits=[("dQb0", P.val("dQb0")), ("dQb1", P.val("dQb1")),
                  ("sVe", P.val("sVe")), ("sQKe", P.val("sQKe"))])
    P.check_deadlock("block")
    with nc.Block() as blk:
        P.replay(blk, sems)
    ps1.close()
    s1a.close()

    s1b = ExitStack()
    masks_sb = s1b.enter_context(nc.sbuf_tensor("masks_sb", [128, 2048], F32))
    Qs = s1b.enter_context(nc.sbuf_tensor("Qs", [128, 3 * 512], BF16))
    PT_sb = s1b.enter_context(nc.sbuf_tensor("PT_sb", [128, 3 * 2048], BF16))
    OTbf = s1b.enter_context(nc.sbuf_tensor("OTbf", [128, 2 * 512], BF16))
    mparts = s1b.enter_context(nc.sbuf_tensor("mparts", [128, 64], F32))
    OTf = s1b.enter_context(nc.sbuf_tensor("OTf", [128, 512], F32))
    lrow = s1b.enter_context(nc.sbuf_tensor("lrow", [1, 512], F32))
    l4 = s1b.enter_context(nc.sbuf_tensor("l4", [128, 4], F32))
    rl4 = s1b.enter_context(nc.sbuf_tensor("rl4", [128, 4], F32))
    rlrow = s1b.enter_context(nc.sbuf_tensor("rlrow", [1, 512], F32))
    rlb = s1b.enter_context(nc.sbuf_tensor("rlb", [128, 512], F32))
    wstage_f = s1b.enter_context(nc.sbuf_tensor("wstage_f", [128, 2 * 2048], F32))
    wstage_b = s1b.enter_context(nc.sbuf_tensor("wstage_b", [128, 2 * 2048], BF16))
    ps2 = ExitStack()
    pP = [ps2.enter_context(nc.psum_tensor(f"pP{i}", [128, 512], F32)) for i in range(6)]
    pOT = ps2.enter_context(nc.psum_tensor("pOT", [128, 512], F32))
    pL = ps2.enter_context(nc.psum_tensor("pL", [1, 512], F32))

    P.emit("sync", lambda e, s: e.dma_start(out=masks_sb[:], in_=masks), inc=("dMask", 16))

    nw = [0]

    def wcast(groups, sf=None, sb=None, sw=2048):
        for (wsrc, dst, R_, C_) in groups:
            width = min(C_, sw)
            for r in range(R_ // 128):
                for cb in range(max(1, C_ // width)):
                    slot = nw[0] % 2
                    sg = f"dG{slot}"
                    stf = sf if sf is not None else wstage_f
                    stb = sb if sb is not None else wstage_b
                    P.emit("gpsimd", (lambda wsrc, r, cb, width, slot, stf=stf, sw=sw: lambda e, s: e.dma_start(
                        out=stf[:, slot * sw: slot * sw + width],
                        in_=wsrc[r * 128:(r + 1) * 128, cb * width:(cb + 1) * width]))(wsrc, r, cb, width, slot),
                        inc=(sg, 16))
                    P.emit("gpsimd", (lambda width, slot, stf=stf, stb=stb, sw=sw: lambda e, s: e.tensor_copy(
                        stb[:, slot * sw: slot * sw + width],
                        stf[:, slot * sw: slot * sw + width]))(width, slot),
                        waits=[(sg, P.val(sg))])
                    P.emit("gpsimd", lambda e, s: e.drain())
                    P.emit("gpsimd", (lambda dst, r, cb, width, slot, stb=stb, sw=sw: lambda e, s: e.dma_start(
                        out=dst[r * 128:(r + 1) * 128, cb * width:(cb + 1) * width],
                        in_=stb[:, slot * sw: slot * sw + width]))(dst, r, cb, width, slot),
                        inc=(sg, 16))
                    nw[0] += 1

    wcast([(wproj, wp_bf, C, C)])
    dG_wp = (P.val("dG0"), P.val("dG1"))
    wcast([(wgT, wg_bf, C, INTER), (wuT, wu_bf, C, INTER)])
    dG_gu = (P.val("dG0"), P.val("dG1"))

    sPgrp_after_p2 = {}
    nP = 0
    kbg = 0
    agdma = []
    consumers = []
    CSTAB = -14.0

    def q_strip_dma(itn):
        j, h = ITS2[itn]
        w = []
        if itn >= 3:
            w.append(("sP", sPgrp_after_p2[itn - 3]))
        P.emit("scalar", (lambda j, h, itn: lambda e, s: e.dma_start(
            out=Qs[:, (itn % 3) * 512:(itn % 3 + 1) * 512],
            in_=q_scr[h * 128:(h + 1) * 128, j * 512:(j + 1) * 512]))(j, h, itn),
            waits=w, inc=(f"dC{itn % 3}", 16))

    JS2 = [0, 1, 4, 5, 8, 9, 12, 13, 2, 3, 6, 7, 10, 11, 14, 15]
    ITS2 = [(j, h) for j in JS2 for h in range(4)]
    q_strip_dma(0)
    q_strip_dma(1)

    def p2_mm(it, kb, j, h):
        nonlocal nP
        banks = []
        base = P.val("sPd2")
        for kt in range(4):
            bank = pP[nP % 6]
            banks.append(bank)
            w = []
            if nP >= 6:
                w.append(consumers[nP - 6])
            if kb == 0 and kt == 0:
                w.append((f"dC{it % 3}", 16 * (it // 3 + 1)))
                if it == 0:
                    w.append(("dMask", 16))
            P.emit("tensor", (lambda bank, kt, h, kb, it: lambda e, s: e.matmul(
                bank[:], K_sb[:, h * T + kb * 512 + kt * 128: h * T + kb * 512 + kt * 128 + 128],
                Qs[:, (it % 3) * 512:(it % 3 + 1) * 512],
                start=True, stop=True))(bank, kt, h, kb, it),
                waits=w, inc=("sP", 1))
            consumers.append(("sPd2", base + kt + 1))
            nP += 1
        if kb == j:
            sPv0 = P.val("sP") - 4
            for kt in range(4):
                P.emit("vector", (lambda bank, kt: lambda e, s: e.tensor_tensor(
                    bank[:], bank[:], masks_sb[:, kt * 512:(kt + 1) * 512],
                    OP.add))(banks[kt], kt),
                    waits=[("sP", sPv0 + kt + 1)],
                    inc=("sMask", 1))
        return banks

    def p2_soft(it, kb, banks, sPv0, j, h):
        nonlocal kbg
        for kt in range(4):
            w = [("sP", sPv0 + kt + 1)]
            if kb == j:
                w.append(("sMask", P.val("sMask") - 3 + kt))
            if kt == 0:
                if kbg >= 3:
                    w.append(("sAVd", kbg - 2))
            P.emit("scalar", (lambda bank, kt, kbg: lambda e, s: e.activation(
                PT_sb[:, (kbg % 3) * 2048 + kt * 512:(kbg % 3) * 2048 + (kt + 1) * 512],
                bank[:], AF.Exp, bias=cstab_sb[:], scale=1.0))(banks[kt], kt, kbg),
                waits=w, inc=("sPd2", 1))
        sPdv = P.val("sPd2")
        for kt in range(4):
            ttg = kb * 4 + kt
            w = [("sPd2", sPdv - 3 + kt)]
            if kt == 0 and kb == 0:
                if it >= 1:
                    w.append(("sOTe", it))
                    w.append(("sRlrow", it))
            P.emit("tensor", (lambda kt, ttg, h, kb, j, kbg: lambda e, s: e.matmul(
                pOT[:], V_sb[:, ttg * FPC + h * 128: ttg * FPC + (h + 1) * 128],
                PT_sb[:, (kbg % 3) * 2048 + kt * 512:(kbg % 3) * 2048 + (kt + 1) * 512],
                start=(kb == 0 and kt == 0), stop=(kb == j and kt == 3),
                skip_group_check=True))(kt, ttg, h, kb, j, kbg),
                waits=w, inc=("sAVd", 1) if kt == 3 else None)
            P.emit("tensor", (lambda kt, kb, j, kbg: lambda e, s: e.matmul(
                pL[:], ones_bf[:, 0:1],
                PT_sb[:, (kbg % 3) * 2048 + kt * 512:(kbg % 3) * 2048 + (kt + 1) * 512],
                start=(kb == 0 and kt == 0), stop=(kb == j and kt == 3),
                skip_group_check=True))(kt, kb, j, kbg),
                inc=("sLd", 1) if (kb == j and kt == 3) else None)
        kbg += 1

    def p2_tail(it, j, h):
        sAVv = P.val("sAVd")
        P.emit("vector", lambda e, s: e.tensor_copy(OTf[:], pOT[:]),
               waits=[("sAVd", sAVv)])
        P.emit("vector", lambda e, s: e.drain(), inc=("sOTf", 1))
        P.emit("vector", lambda e, s: e.tensor_copy(lrow[:], pL[:]),
               waits=[("sLd", it + 1)])
        P.emit("vector", lambda e, s: e.drain())
        P.emit("vector", lambda e, s: e.reciprocal(rlrow[:], lrow[:]))
        P.emit("vector", lambda e, s: e.drain(), inc=("sRlrow", 1))
        P.emit("tensor", lambda e, s: e.matmul(pOT[:], ones32[0:1, :], rlrow[0:1, :],
                                               start=True, stop=True),
               waits=[("sRlrow", it + 1), ("sOTf", it + 1)], inc=("sBc", 1))
        P.emit("vector", lambda e, s: e.tensor_copy(rlb[:], pOT[:]),
               waits=[("sBc", P.val("sBc"))])
        P.emit("vector", lambda e, s: e.drain())
        w = []
        if it >= 2:
            w.append((f"dAg{it % 2}", agdma[it - 2]))
        P.emit("vector", (lambda it: lambda e, s: e.tensor_tensor(
            OTbf[:, (it % 2) * 512:(it % 2 + 1) * 512], OTf[:], rlb[:],
            OP.mult))(it),
            waits=w, inc=("sOTe", 1))
        P.emit("sync", (lambda it, j, h: lambda e, s: e.dma_start(
            out=ag_in[(j // 2) * 512 + h * 128:(j // 2) * 512 + (h + 1) * 128,
                      (j % 2) * 512:(j % 2 + 1) * 512],
            in_=OTbf[:, (it % 2) * 512:(it % 2 + 1) * 512]))(it, j, h),
            waits=[("sOTe", it + 1)], inc=(f"dAg{it % 2}", 16))
        agdma.append(P.val(f"dAg{it % 2}"))

    JS = [0, 1, 4, 5, 8, 9, 12, 13, 2, 3, 6, 7, 10, 11, 14, 15]
    ITS = [(j, h) for j in JS for h in range(4)]
    ag_counts = []
    for it in range(64):
        j, h = ITS[it]
        for k in range(j + 1):
            sPv0 = P.val("sP")
            banks2 = p2_mm(it, k, j, h)
            p2_soft(it, k, banks2, sPv0, j, h)
        sPgrp_after_p2[it] = P.val("sP")
        p2_tail(it, j, h)
        ag_counts.append((P.val("dAg0"), P.val("dAg1")))
        if it < 62:
            q_strip_dma(it + 2)

    def emit_ag(bq):
        last_idx = max(i for i, (j, h) in enumerate(ITS) if j // 2 == bq)
        c0, c1 = ag_counts[last_idx]
        P.emit("gpsimd", (lambda bq: lambda e, s: e.collective_compute(
            "AllGather", OP.bypass,
            replica_groups=[[0, 1, 2, 3], [4, 5, 6, 7]],
            ins=[ag_in[bq * 512:(bq + 1) * 512, :].opt()],
            outs=[ag_out[bq * 2048:(bq + 1) * 2048, :].opt()]))(bq),
            waits=[("dAg0", c0), ("dAg1", c1)], inc=("cc", 1))

    for bq in (0, 2, 4, 6):
        emit_ag(bq)
    P.emit("sync", lambda e, s: e.nop(), waits=[("dAg0", 32 * 16), ("dAg1", 32 * 16),
                                                ("dMask", 16)])
    P.emit("scalar", lambda e, s: e.nop(), waits=[("dC0", P.val("dC0")), ("dC1", P.val("dC1"))])
    P.check_deadlock("block")
    with nc.Block() as blk:
        P.replay(blk, sems)
    ps2.close()
    s1b.close()
    s1.close()

    s2a = ExitStack()
    y1TS = s2a.enter_context(nc.sbuf_tensor("y1TS", [128, 2 * 2048], BF16))
    wp_sb = s2a.enter_context(nc.sbuf_tensor("wp_sb", [128, 16 * 2048], BF16))
    OTg = s2a.enter_context(nc.sbuf_tensor("OTg", [128, 2 * 2048], BF16))
    xpb = s2a.enter_context(nc.sbuf_tensor("xpb", [128, 2048], F32))
    bpb = s2a.enter_context(nc.sbuf_tensor("bpb", [128, 2048], F32))
    y1t = s2a.enter_context(nc.sbuf_tensor("y1t", [128, 2 * 2048], F32))
    y1b = s2a.enter_context(nc.sbuf_tensor("y1b", [128, 2048], BF16))
    junk2 = s2a.enter_context(nc.sbuf_tensor("junk2", [128, 2048], BF16))
    fstage2 = s2a.enter_context(nc.sbuf_tensor("fstage2", [128, 2048], F32))
    bpr = s2a.enter_context(nc.sbuf_tensor("bpr", [1, 2048], F32))
    rwbf = s2a.enter_context(nc.sbuf_tensor("rwbf", [128, 128], BF16))
    eq8 = s2a.enter_context(nc.sbuf_tensor("eq8", [128, 8], F32))
    msk8 = s2a.enter_context(nc.sbuf_tensor("msk8", [128, 8], F32))
    sc1 = s2a.enter_context(nc.sbuf_tensor("sc1", [128, 8], F32))
    vs1 = s2a.enter_context(nc.sbuf_tensor("vs1", [128, 2], F32))
    ps3 = ExitStack()
    pp = [ps3.enter_context(nc.psum_tensor(f"pp{i}", [128, 512], F32)) for i in range(2)]
    prt = ps3.enter_context(nc.psum_tensor("prt", [128, 8], F32))
    pyt = ps3.enter_context(nc.psum_tensor("pyt", [128, 2048], BF16))

    dynoff = {}

    def setup_dynoff(e, s):
        reg = e.register("r_qr").__enter__()
        e.reg_mov(reg, e.partition_id())
        e.reg_alu(reg, reg, 3, OP.bitwise_and)
        e.reg_alu(reg, reg, 2, OP.mult)
        dynoff[0] = e.snap(reg)
        e.reg_alu(reg, reg, 1, OP.add)
        dynoff[1] = e.snap(reg)
        return None
    P.ops["sync"].append(lambda e, s: setup_dynoff(e, s))

    P.emit("sync", lambda e, s: e.dma_start(out=bpr[:], in_=bp_row), inc=("dInit", 16))
    for fb in range(4):
        w = [("dInit", 48)] if fb == 0 else []
        if fb >= 2:
            w.append(("sPPe", fb - 1))
        P.emit("tensor", (lambda fb: lambda e, s: e.matmul(
            pp[fb % 2][:], ones32[0:1, :], bpr[0:1, fb * 512:(fb + 1) * 512],
            start=True, stop=True))(fb),
            waits=w, inc=("sPP", 1))
        P.emit("vector", (lambda fb: lambda e, s: e.tensor_copy(
            bpb[:, fb * 512:(fb + 1) * 512], pp[fb % 2][:]))(fb),
            waits=[("sPP", P.val("sPP"))], inc=("sPPe", 1))
    for bq in (1, 3, 5, 7):
        emit_ag(bq)
    P.emit("sync", lambda e, s: e.dma_start(
        out=wp_sb[:].rearrange("p (c f) -> p c f", c=16),
        in_=wp_bf.rearrange("(c p) f -> p c f", p=128)),
        waits=[("dG0", dG_wp[0]), ("dG1", dG_wp[1])], inc=("dWp", 16))
    P.emit("sync", lambda e, s: e.dma_start(
        out=fstage2[:, 0:128].rearrange("p (c e) -> p c e", c=16),
        in_=rwT.rearrange("(c p) e -> p c e", p=128)),
        inc=("dF2", 16))
    P.emit("vector", lambda e, s: e.tensor_copy(rwbf[:], fstage2[:, 0:128]),
           waits=[("dF2", P.val("dF2"))], inc=("sW", 1))

    y1dma = []
    for tt in range(16):
        w = [("cc", 4)] if tt == 0 else ([("cc", 8)] if tt == 8 else [])
        if tt >= 2:
            w.append(("sPP", 4 + 4 * (tt - 1)))
        P.emit("sync", (lambda tt: lambda e, s: e.dma_start(
            out=OTg[:, (tt % 2) * 2048:(tt % 2 + 1) * 2048].rearrange(
                "p (q c t) -> p q c t", q=1, c=16),
            in_=ag_out.rearrange("(q c p) t -> p q c t", p=128, c=16)[
                :, bass.ds(dynoff[tt // 8], 1), :, (tt % 8) * 128:(tt % 8 + 1) * 128]))(tt),
            waits=w, inc=(f"dOg{tt % 2}", 16))
        dv_otg = (f"dOg{tt % 2}", P.val(f"dOg{tt % 2}"))
        P.emit("sync", (lambda tt: lambda e, s: e.dma_start(
            out=fstage2[:], in_=x_own[tt * 128:(tt + 1) * 128, :]))(tt),
            waits=[("sXo", tt), ("sW", 33)], inc=("dF2", 16))
        w = [("dF2", P.val("dF2")), ("sPPe", 4)]
        if tt >= 1:
            w.append(("sPPe", 4 + 4 * tt))
        P.emit("vector", lambda e, s: e.tensor_tensor(xpb[:], fstage2[:], bpb[:], OP.add),
               waits=w)
        P.emit("vector", lambda e, s: e.drain(), inc=("sXo", 1))
        for fb in range(4):
            gp = P.val("sPP")
            bank = pp[gp % 2]
            for dc in range(16):
                w = []
                if dc == 0:
                    w.append(dv_otg)
                    if gp >= 6:
                        w.append(("sPPe", gp - 1))
                P.emit("tensor", (lambda dc, fb, tt, bank: lambda e, s: e.matmul(
                    bank[:], OTg[:, (tt % 2) * 2048 + dc * 128:(tt % 2) * 2048 + dc * 128 + 128],
                    wp_sb[:, dc * 2048 + fb * 512: dc * 2048 + fb * 512 + 512],
                    start=(dc == 0), stop=(dc == 15)))(dc, fb, tt, bank),
                    waits=w, inc=("sPP", 1) if dc == 15 else None)
            w = [("sPP", P.val("sPP")), ("sXo", tt + 1)]
            if fb == 0:
                if tt >= 2:
                    w.append((f"dY{tt % 2}", y1dma[tt - 2]))
                if tt >= 1:
                    w.append(("sYc", tt))
            P.emit("vector", (lambda fb, tt, bank: lambda e, s: e.tensor_tensor(
                y1t[:, (tt % 2) * 2048 + fb * 512:(tt % 2) * 2048 + (fb + 1) * 512],
                bank[:], xpb[:, fb * 512:(fb + 1) * 512], OP.add))(fb, tt, bank),
                waits=w, inc=("sPPe", 1))
        y1v = y1t[:, (tt % 2) * 2048:(tt % 2 + 1) * 2048]
        P.emit("vector", lambda e, s: e.drain())
        P.emit("vector", (lambda y1v: lambda e, s: e.reduce_sum(
            sc1[:, 6:7], y1v, axis=AX.X, negate=True))(y1v))
        P.emit("vector", lambda e, s: e.drain())
        P.emit("vector", lambda e, s: e.tensor_scalar(sc1[:, 6:7], sc1[:, 6:7],
                                                      1.0 / 2048.0, None, OP.mult))
        P.emit("vector", lambda e, s: e.drain(), inc=("sNm", 1))
        P.emit("scalar", (lambda y1v: lambda e, s: e.activation(
            junk2[:], y1v, AF.Square, bias=sc1[:, 6:7], scale=1.0,
            accum_out=vs1[:, 0:1]))(y1v),
            waits=[("sNm", tt + 1)], inc=("sSq", 1))
        P.emit("vector", lambda e, s: e.tensor_scalar(vs1[:, 1:2], vs1[:, 0:1],
                                                      1.0 / 2048.0, EPS, OP.mult, OP.add),
               waits=[("sSq", P.val("sSq"))])
        P.emit("vector", lambda e, s: e.drain(), inc=("sV2", 1))
        P.emit("scalar", lambda e, s: e.activation(sc1[:, 5:6], vs1[:, 1:2], AF.Ln),
               waits=[("sV2", P.val("sV2"))])
        P.emit("scalar", lambda e, s: e.drain())
        P.emit("scalar", lambda e, s: e.activation(sc1[:, 4:5], sc1[:, 5:6], AF.Exp,
                                                   scale=-0.5))
        P.emit("scalar", lambda e, s: e.drain(), inc=("sRstd", 1))
        P.emit("vector", (lambda y1v: lambda e, s: e.tensor_scalar(
            y1v, y1v, sc1[:, 6:7], sc1[:, 4:5], OP.add, OP.mult))(y1v),
            waits=[("sRstd", P.val("sRstd"))])
        P.emit("vector", lambda e, s: e.drain(), inc=("sY1", 1))
        P.emit("sync", (lambda tt, y1v: lambda e, s: e.dma_start(
            out=y1_scr[tt * 128:(tt + 1) * 128, :], in_=y1v))(tt, y1v),
            waits=[("sY1", tt + 1)], inc=(f"dY{tt % 2}", 16))
        y1dma.append(P.val(f"dY{tt % 2}"))
        w = [("sYt", tt)] if tt >= 1 else []
        P.emit("vector", (lambda y1v: lambda e, s: e.tensor_copy(y1b[:], y1v))(y1v), waits=w)
        P.emit("vector", lambda e, s: e.drain(), inc=("sYc", tt + 1 - P.val("sYc")))
        for hc in range(16):
            w = [("sYc", tt + 1)] if hc == 0 else []
            if hc == 0 and tt >= 1:
                w.append(("sYe", tt))
            P.emit("tensor", (lambda hc: lambda e, s: e.transpose(
                pyt[:, hc * 128:(hc + 1) * 128], y1b[:, hc * 128:(hc + 1) * 128],
                ident_bf[:]))(hc),
                waits=w, inc=("sYt", 1) if hc == 15 else None)
        w = [("sYt", tt + 1)]
        if tt >= 2:
            w.append((f"dYT{tt % 2}", 16 * (tt // 2)))
            w.append(("sRt", tt - 1))
        P.emit("vector", (lambda tt: lambda e, s: e.tensor_copy(
            y1TS[:, (tt % 2) * 2048:(tt % 2 + 1) * 2048], pyt[:]))(tt),
            waits=w, inc=("sYe", 1))
        P.emit("sync", (lambda tt: lambda e, s: e.dma_start(
            out=y1T_scr.rearrange("(c p) t -> p c t", p=128)[:, :, tt * 128:(tt + 1) * 128],
            in_=y1TS[:, (tt % 2) * 2048:(tt % 2 + 1) * 2048].rearrange("p (c n) -> p c n", c=16)))(tt),
            waits=[("sYe", tt + 1)], inc=(f"dYT{tt % 2}", 16))
        for hc in range(16):
            w = [("sYe", tt + 1)] if hc == 0 else []
            if hc == 0 and tt >= 1:
                w.append(("sRt", tt))
            P.emit("tensor", (lambda hc, tt: lambda e, s: e.matmul(
                prt[:], y1TS[:, (tt % 2) * 2048 + hc * 128:(tt % 2) * 2048 + hc * 128 + 128],
                rwbf[:, hc * 8:(hc + 1) * 8],
                start=(hc == 0), stop=(hc == 15)))(hc, tt),
                waits=w, inc=("sBc", 1) if hc == 15 else None)
        P.emit("vector", lambda e, s: e.reduce_max(sc1[:, 0:1], prt[:], axis=AX.X),
               waits=[("sBc", P.val("sBc"))])
        P.emit("vector", lambda e, s: e.drain())
        P.emit("vector", lambda e, s: e.tensor_scalar(eq8[:], prt[:], sc1[:, 0:1], None, OP.is_ge))
        P.emit("vector", lambda e, s: e.drain())
        P.emit("vector", lambda e, s: e.scalar_tensor_tensor(
            msk8[:], eq8[:], NEG, prt[:], OP.mult, OP.add))
        P.emit("vector", lambda e, s: e.drain(), inc=("sRt", 1))
        P.emit("vector", lambda e, s: e.reduce_max(sc1[:, 1:2], msk8[:], axis=AX.X))
        P.emit("vector", lambda e, s: e.drain())
        P.emit("vector", lambda e, s: e.tensor_scalar(sc1[:, 1:2], sc1[:, 1:2],
                                                      sc1[:, 0:1], None, OP.subtract))
        P.emit("vector", lambda e, s: e.drain(), inc=("sV2", 1))
        P.emit("scalar", lambda e, s: e.activation(sc1[:, 2:3], sc1[:, 1:2], AF.Exp),
               waits=[("sV2", P.val("sV2"))], inc=("sSq", 1))
        P.emit("vector", lambda e, s: e.tensor_scalar(sc1[:, 3:4], sc1[:, 2:3], 1.0,
                                                      None, OP.add),
               waits=[("sSq", P.val("sSq"))])
        P.emit("vector", lambda e, s: e.drain())
        P.emit("vector", lambda e, s: e.reciprocal(sc1[:, 7:8], sc1[:, 3:4]))
        P.emit("vector", lambda e, s: e.drain())
        P.emit("vector", (lambda tt: lambda e, s: e.scalar_tensor_tensor(
            comb_sb[:, tt:tt + 1], sc1[:, 2:3], sc1[:, 7:8], sc1[:, 7:8],
            OP.mult, OP.add))(tt))
        P.emit("vector", lambda e, s: e.drain())
    P.emit("sync", lambda e, s: e.nop(),
           waits=[("dY0", P.val("dY0")), ("dY1", P.val("dY1")),
                  ("dOg0", P.val("dOg0")), ("dOg1", P.val("dOg1")),
                  ("dF2", P.val("dF2")), ("dYT0", P.val("dYT0")),
                  ("dYT1", P.val("dYT1"))])
    P.check_deadlock("block")
    with nc.Block() as blk:
        P.replay(blk, sems)
    ps3.close()
    s2a.close()

    s2b = ExitStack()
    guT_sb = s2b.enter_context(nc.sbuf_tensor("guT_sb", [128, 64 * 512], BF16))
    guTr_sb = s2b.enter_context(nc.sbuf_tensor("guTr_sb", [128, 8 * 512], BF16))
    y1ts = s2b.enter_context(nc.sbuf_tensor("y1ts", [128, 2 * 16 * 512], BF16))
    wgt_i = s2b.enter_context(nc.sbuf_tensor("wgt_i", [128, 2 * 2048], BF16))
    wut_i = s2b.enter_context(nc.sbuf_tensor("wut_i", [128, 2 * 2048], BF16))
    wdt = s2b.enter_context(nc.sbuf_tensor("wdt", [128, 4 * 512], BF16))
    dsh = s2b.enter_context(nc.sbuf_tensor("dsh", [128, 4 * 2048], F32))
    sgv = s2b.enter_context(nc.sbuf_tensor("sgv", [128, 2 * 512], BF16))
    y1rb = s2b.enter_context(nc.sbuf_tensor("y1rb", [128, 2048], F32))
    y2 = s2b.enter_context(nc.sbuf_tensor("y2", [128, 2 * 2048], F32))
    ln2s = s2b.enter_context(nc.sbuf_tensor("ln2s", [128, 4], F32))
    junk3 = s2b.enter_context(nc.sbuf_tensor("junk3", [128, 2048], BF16))
    ps4 = ExitStack()
    pgA = [ps4.enter_context(nc.psum_tensor(f"pgA{i}", [128, 512], F32)) for i in range(2)]
    puA = [ps4.enter_context(nc.psum_tensor(f"puA{i}", [128, 512], F32)) for i in range(2)]
    pd = [ps4.enter_context(nc.psum_tensor(f"pd{i}", [128, 512], F32)) for i in range(4)]

    wstage_f2 = s2b.enter_context(nc.sbuf_tensor("wstage_f2", [128, 2 * 1024], F32))
    wstage_b2 = s2b.enter_context(nc.sbuf_tensor("wstage_b2", [128, 2 * 1024], BF16))
    wcast([(rgT, rg_bf, C, MOE), (ruT, ru_bf, C, MOE)], wstage_f2, wstage_b2, sw=1024)
    dG_r = (P.val("dG0"), P.val("dG1"))
    wcast([(rdT, rd_bf, MOE, C), (wdT, wd_bf, INTER, C)], wstage_f2, wstage_b2, sw=1024)
    dG_wcast = (P.val("dG0"), P.val("dG1"))

    icA = 0
    icg = 0
    mma_after_tb = []
    icg_after_tb = 0
    for tbF in range(4):
        w = []
        if tbF >= 2:
            w.append(("sMMa", mma_after_tb[tbF - 2]))
        P.emit("sync", (lambda tbF: lambda e, s: e.dma_start(
            out=y1ts[:, (tbF % 2) * 8192:(tbF % 2 + 1) * 8192].rearrange("p (c t) -> p c t", c=16),
            in_=y1T_scr.rearrange("(c p) t -> p c t", p=128)[:, :, tbF * 512:(tbF + 1) * 512]))(tbF),
            waits=w + [("dYT0", 8 * 16), ("dYT1", 8 * 16)], inc=(f"dYS{tbF % 2}", 16))
        dv_ys = (f"dYS{tbF % 2}", P.val(f"dYS{tbF % 2}"))
        for routed in (0, 1):
            nic = (MOE if routed else INTER) // 128
            wg_src, wu_src = (rg_bf, ru_bf) if routed else (wg_bf, wu_bf)
            gdst = guTr_sb if routed else guT_sb
            for ic in range(nic):
                slot = icA % 2
                w = [("dG0", dG_gu[0]), ("dG1", dG_gu[1])] if icA == 0 else []
                if routed and ic == 0 and tbF == 0:
                    w += [("dG0", dG_r[0]), ("dG1", dG_r[1])]
                if icA >= 2:
                    w.append(("sMMa", 2 * (icA - 2) + 2))
                P.emit("sync", (lambda wg_src, ic, slot: lambda e, s: e.dma_start(
                    out=wgt_i[:, slot * 2048:(slot + 1) * 2048].rearrange("p (c i) -> p c i", c=16),
                    in_=wg_src.rearrange("(c p) i -> p c i", p=128)[:, :, ic * 128:(ic + 1) * 128]))(wg_src, ic, slot),
                    waits=w, inc=(f"dWg{slot}", 16))
                P.emit("sync", (lambda wu_src, ic, slot: lambda e, s: e.dma_start(
                    out=wut_i[:, slot * 2048:(slot + 1) * 2048].rearrange("p (c i) -> p c i", c=16),
                    in_=wu_src.rearrange("(c p) i -> p c i", p=128)[:, :, ic * 128:(ic + 1) * 128]))(wu_src, ic, slot),
                    inc=(f"dWg{slot}", 16))
                dv_w = (f"dWg{slot}", P.val(f"dWg{slot}"))
                for gu_i, (wsb, bankp) in enumerate(((wgt_i, pgA), (wut_i, puA))):
                    bank = bankp[icA % 2]
                    for hc in range(16):
                        w = []
                        if hc == 0:
                            w.append(dv_w)
                            w.append(dv_ys)
                            if gu_i == 0 and icA >= 2:
                                w.append(("sSg", icA - 1))
                            if gu_i == 1 and icA >= 2:
                                w.append(("sGu", icA - 1))
                        P.emit("tensor", (lambda hc, bank, wsb, slot, tbF: lambda e, s: e.matmul(
                            bank[:], wsb[:, slot * 2048 + hc * 128: slot * 2048 + hc * 128 + 128],
                            y1ts[:, (tbF % 2) * 8192 + hc * 512:(tbF % 2) * 8192 + (hc + 1) * 512],
                            start=(hc == 0), stop=(hc == 15)))(hc, bank, wsb, slot, tbF),
                            waits=w, inc=("sMMa", 1) if hc == 15 else None)
                w = [("sMMa", P.val("sMMa") - 1)]
                if icA >= 2:
                    w.append(("sGu", icA - 1))
                P.emit("scalar", (lambda icA, slot: lambda e, s: e.activation(
                    sgv[:, slot * 512:(slot + 1) * 512], pgA[icA % 2][:], AF.Silu))(icA, slot),
                    waits=w, inc=("sSg", 1))
                w = [("sSg", P.val("sSg")), ("sMMa", P.val("sMMa"))]
                if routed == 0 and ic == 0 and tbF >= 1:
                    w.append(("sDm", icg_after_tb))
                P.emit("vector", (lambda icA, ic, slot, gdst: lambda e, s: e.tensor_tensor(
                    gdst[:, ic * 512:(ic + 1) * 512], sgv[:, slot * 512:(slot + 1) * 512],
                    puA[icA % 2][:], OP.mult))(icA, ic, slot, gdst),
                    waits=w)
                P.emit("vector", lambda e, s: e.drain(), inc=("sGu", 1))
                icA += 1
        mma_after_tb.append(P.val("sMMa"))
        sGuW = P.val("sGu")
        for routed in (0, 1):
            nic = (MOE if routed else INTER) // 128
            wsrc = rd_bf if routed else wd_bf
            gsrc = guTr_sb if routed else guT_sb
            for hb in range(4):
                sDe_hb = P.val("sDe")
                for ic in range(nic):
                    slot = icg % 4
                    w = []
                    if icg == 0 and tbF == 0:
                        w += [("dG0", dG_wcast[0]), ("dG1", dG_wcast[1])]
                    if icg >= 4:
                        w.append(("sDm", icg - 3))
                    P.emit("sync", (lambda wsrc, ic, hb, slot: lambda e, s: e.dma_start(
                        out=wdt[:, slot * 512:(slot + 1) * 512],
                        in_=wsrc[ic * 128:(ic + 1) * 128, hb * 512:(hb + 1) * 512]))(wsrc, ic, hb, slot),
                        waits=w, inc=(f"dDt{slot}", 16))
                    dv_w = (f"dDt{slot}", P.val(f"dDt{slot}"))
                    for tt4 in range(4):
                        w = []
                        if tt4 == 0:
                            w.append(dv_w)
                        if ic == 0 and tt4 == 0:
                            w.append(("sGu", sGuW))
                            w.append(("sDe", sDe_hb))
                        if ic == 0 and tt4 == 0 and hb == 0 and routed == 0:
                            w.append(("sT2", 8 * tbF))
                        P.emit("tensor", (lambda ic, hb, tt4, slot, gsrc, nic: lambda e, s: e.matmul(
                            pd[tt4][:], gsrc[:, ic * 512 + tt4 * 128: ic * 512 + tt4 * 128 + 128],
                            wdt[:, slot * 512:(slot + 1) * 512],
                            start=(ic == 0), stop=(ic == nic - 1),
                            skip_group_check=True))(ic, hb, tt4, slot, gsrc, nic),
                            waits=w, inc=("sDm", 1) if tt4 == 3 else None)
                    icg += 1
                sDmv = P.val("sDm")
                for tt4 in range(4):
                    if routed == 0:
                        P.emit("scalar", (lambda tt4, hb: lambda e, s: e.copy(
                            dsh[:, tt4 * 2048 + hb * 512: tt4 * 2048 + (hb + 1) * 512],
                            pd[tt4][:]))(tt4, hb),
                            waits=[("sDm", sDmv)] if tt4 == 0 else [], inc=("sDe", 1))
                    else:
                        tg = tbF * 4 + tt4
                        P.emit("vector", (lambda tt4, hb, tg: lambda e, s: e.scalar_tensor_tensor(
                            dsh[:, tt4 * 2048 + hb * 512: tt4 * 2048 + (hb + 1) * 512],
                            pd[tt4][:], comb_sb[:, tg:tg + 1],
                            dsh[:, tt4 * 2048 + hb * 512: tt4 * 2048 + (hb + 1) * 512],
                            OP.mult, OP.add))(tt4, hb, tg),
                            waits=[("sDm", sDmv), ("sDe", P.val("sDe"))] if tt4 == 0
                                  else [("sDe", P.val("sDe"))])
                        P.emit("vector", lambda e, s: e.drain(), inc=("sDe", 1))
        icg_after_tb = P.val("sDm")
        for tt4 in range(4):
            tg = tbF * 4 + tt4
            P.emit("sync", (lambda tg: lambda e, s: e.dma_start(
                out=y1rb[:], in_=y1_scr[tg * 128:(tg + 1) * 128, :]))(tg),
                waits=[("sT2", 2 * tg), ("dY0", 8 * 16), ("dY1", 8 * 16)], inc=("dYr", 16))
            y2v = y2[:, (tg % 2) * 2048:(tg % 2 + 1) * 2048]
            w = [("dYr", P.val("dYr")), ("sDe", P.val("sDe"))]
            if tg >= 2:
                w.append((f"dOut{tg % 2}", 16 * (tg // 2)))
            P.emit("vector", (lambda tt4, y2v: lambda e, s: e.tensor_tensor(
                y2v, dsh[:, tt4 * 2048:(tt4 + 1) * 2048], y1rb[:], OP.add))(tt4, y2v),
                waits=w)
            P.emit("vector", lambda e, s: e.drain(), inc=("sT2", 1))
            P.emit("vector", (lambda y2v: lambda e, s: e.reduce_sum(
                ln2s[:, 0:1], y2v, axis=AX.X, negate=True))(y2v))
            P.emit("vector", lambda e, s: e.drain())
            P.emit("vector", lambda e, s: e.tensor_scalar(ln2s[:, 0:1], ln2s[:, 0:1],
                                                          1.0 / 2048.0, None, OP.mult))
            P.emit("vector", lambda e, s: e.drain(), inc=("sT2", 1))
            P.emit("scalar", (lambda y2v, tg: lambda e, s: e.activation(
                junk3[:], y2v, AF.Square, bias=ln2s[:, 0:1], scale=1.0,
                accum_out=ln2s[:, 1:2]))(y2v, tg),
                waits=[("sT2", 2 * tg + 2)], inc=("sSq", 1))
            P.emit("vector", lambda e, s: e.tensor_scalar(ln2s[:, 2:3], ln2s[:, 1:2],
                                                          1.0 / 2048.0, EPS, OP.mult, OP.add),
                   waits=[("sSq", P.val("sSq"))])
            P.emit("vector", lambda e, s: e.drain(), inc=("sV2", 1))
            P.emit("scalar", lambda e, s: e.activation(ln2s[:, 3:4], ln2s[:, 2:3], AF.Ln),
                   waits=[("sV2", P.val("sV2"))])
            P.emit("scalar", lambda e, s: e.drain())
            P.emit("scalar", lambda e, s: e.activation(ln2s[:, 3:4], ln2s[:, 3:4], AF.Exp,
                                                       scale=-0.5))
            P.emit("scalar", lambda e, s: e.drain(), inc=("sRstd", 1))
            P.emit("vector", (lambda y2v: lambda e, s: e.tensor_scalar(
                y2v, y2v, ln2s[:, 0:1], ln2s[:, 3:4], OP.add, OP.mult))(y2v),
                waits=[("sRstd", P.val("sRstd"))])
            P.emit("vector", lambda e, s: e.drain(), inc=("sY2e", 1))
            P.emit("sync", (lambda tg, y2v: lambda e, s: e.dma_start(
                out=out[tg * 128:(tg + 1) * 128, :], in_=y2v))(tg, y2v),
                waits=[("sY2e", tg + 1)], inc=(f"dOut{tg % 2}", 16))
    P.emit("sync", lambda e, s: e.nop(), waits=[("dOut0", 8 * 16), ("dOut1", 8 * 16),
                                                ("dYr", P.val("dYr")),
                                                ("dYS0", P.val("dYS0")),
                                                ("dYS1", P.val("dYS1"))])
    P.check_deadlock("block")
    with nc.Block() as blk:
        P.replay(blk, sems)
    ps4.close()
    s2b.close()
    g.close()
    n_instr = {e: "closed" for e in ENGS}
    return nc


_NC_CACHE = None


def _get_nc():
    global _NC_CACHE
    if _NC_CACHE is None:
        _NC_CACHE = build_nc()
    return _NC_CACHE


def _make_masks():
    m = np.zeros((128, 2048), np.float32)
    for kt in range(4):
        for kk in range(128):
            m[kk, kt * 512: kt * 512 + kt * 128 + kk] = NEG
    return m


def kernel(L0_attn_x, L0_attn_w_attn, L0_attn_b_attn, L0_attn_w_proj, L0_attn_b_proj,
           L0_ffn_router_weight, L0_ffn_sh_gate, L0_ffn_sh_up, L0_ffn_sh_down,
           L0_ffn_r_gate, L0_ffn_r_up, L0_ffn_r_down):
    x = np.asarray(L0_attn_x, np.float32)
    w_attn = np.ascontiguousarray(np.asarray(L0_attn_w_attn, np.float32))
    b_attn = np.asarray(L0_attn_b_attn, np.float32)
    w_proj = np.ascontiguousarray(np.asarray(L0_attn_w_proj, np.float32))
    b_proj = np.asarray(L0_attn_b_proj, np.float32)

    xT = [np.ascontiguousarray(x[b].T) for b in range(B)]
    wgT = np.ascontiguousarray(np.asarray(L0_ffn_sh_gate, np.float32).T)
    wuT = np.ascontiguousarray(np.asarray(L0_ffn_sh_up, np.float32).T)
    wdT = np.ascontiguousarray(np.asarray(L0_ffn_sh_down, np.float32).T)
    rgT = np.ascontiguousarray(np.asarray(L0_ffn_r_gate, np.float32).T)
    ruT = np.ascontiguousarray(np.asarray(L0_ffn_r_up, np.float32).T)
    rdT = np.ascontiguousarray(np.asarray(L0_ffn_r_down, np.float32).T)
    rwT = np.ascontiguousarray(np.asarray(L0_ffn_router_weight, np.float32).T)
    bpr = np.ascontiguousarray(b_proj[None, :])
    masks = _make_masks()

    in_maps = []
    for k in range(8):
        b, hg = divmod(k, 4)
        f0 = hg * FPC
        wq = w_attn[:, f0:f0 + FPC]
        wk = w_attn[:, C + f0:C + f0 + FPC]
        wv = w_attn[:, 2 * C + f0:2 * C + f0 + FPC]
        wqkv_np = np.ascontiguousarray(np.concatenate([wq, wk, wv], axis=1))
        bq = b_attn[f0:f0 + FPC].reshape(4, 128).T
        bk = b_attn[C + f0:C + f0 + FPC].reshape(4, 128).T
        bqk_np = np.ascontiguousarray(np.concatenate([bq, bk], axis=1))
        bv = np.ascontiguousarray(b_attn[2 * C + f0:2 * C + f0 + FPC][None, :])
        tok0 = hg * NTOK
        in_maps.append({
            "xT": xT[b], "x_own": np.ascontiguousarray(x[b, tok0:tok0 + NTOK, :]),
            "wqkv": wqkv_np, "bqk": bqk_np, "bv_row": bv,
            "wproj": w_proj, "bp_row": bpr, "rwT": rwT,
            "wgT": wgT, "wuT": wuT, "wdT": wdT,
            "rgT": rgT, "ruT": ruT, "rdT": rdT,
            "masks": masks,
        })
    nc = _get_nc()
    res = run_bass_kernel_spmd(nc, in_maps, core_ids=list(range(8)))
    full = np.empty((B, T, C), np.float32)
    for k in range(8):
        b, hg = divmod(k, 4)
        tok0 = hg * NTOK
        full[b, tok0:tok0 + NTOK, :] = res.results[k]["out"]
    return full


if __name__ == "__main__":
    build_nc()
    print("built ok")


# version 20
# speedup vs baseline: 1.1717x; 1.1717x over previous
import numpy as np
import concourse.bass as bass
import concourse.mybir as mybir
from concourse.bass_utils import run_bass_kernel_spmd
from concourse.alu_op_type import AluOpType as OP
from contextlib import ExitStack

F32 = mybir.dt.float32
BF16 = mybir.dt.bfloat16
I32 = mybir.dt.int32
AF = mybir.ActivationFunctionType
AX = mybir.AxisListType

B, T, C = 2, 8192, 2048
NH, HD, HG = 16, 128, 4
FPC = HG * HD
NTOK = 2048
RSQ = 1.0 / np.sqrt(128.0)
NEG = -1e30
EPS = 1e-5
INTER = 8192
MOE = 1024
ENGS = ("tensor", "vector", "scalar", "gpsimd", "sync")
_DEBUG_TAPS = False


class Plan:
    def __init__(self):
        self.ops = {e: [] for e in ENGS}
        self.cnt = {}
        self.sched = {}
        self.all_sched = {}

    def check_deadlock(self, tag):
        pos = {e: 0 for e in self.sched}
        sem = dict(self.final_sem) if hasattr(self, "final_sem") else {}
        progress = True
        while progress:
            progress = False
            for e, ops in self.sched.items():
                while pos[e] < len(ops):
                    wl, inc = ops[pos[e]]
                    if all(sem.get(s, 0) >= v for s, v in wl):
                        if inc:
                            sem[inc[0]] = sem.get(inc[0], 0) + inc[1]
                        pos[e] += 1
                        progress = True
                    else:
                        break
        stuck = {e: (pos[e], len(ops)) for e, ops in self.sched.items() if pos[e] < len(ops)}
        if stuck:
            print(f"DEADLOCK in {tag}:")
            for e, (p, n) in stuck.items():
                wl, inc = self.sched[e][p]
                unmet = [(s, v, sem.get(s, 0)) for s, v in wl if sem.get(s, 0) < v]
                print(f"  {e}: stuck at op {p}/{n}, unmet={unmet}")
            raise RuntimeError(f"deadlock in {tag}")
        self.final_sem = sem
        self.sched = {}

    def val(self, sem):
        return self.cnt.get(sem, 0)

    def bump(self, sem, k):
        self.cnt[sem] = self.cnt.get(sem, 0) + k
        return self.cnt[sem]

    def emit(self, eng, fn, waits=(), inc=None):
        wl = [(s, v) for (s, v) in waits if v > 0]
        if inc is not None:
            self.bump(*inc)
        self.sched.setdefault(eng, []).append((wl, inc))

        def op(e, sems):
            for s, v in wl:
                e.wait_ge(sems[s], v)
            instr = fn(e, sems)
            if inc is not None:
                assert instr is not None
                instr.then_inc(sems[inc[0]], inc[1])
        self.ops[eng].append(op)

    def replay(self, block, sems):
        for eng in ENGS:
            ops = self.ops[eng]
            if not ops:
                continue

            def body(e, ops=ops):
                for op in ops:
                    op(e, sems)
            getattr(block, eng)(body)
        self.ops = {e: [] for e in ENGS}


SEM_NAMES = [
    "dS", "dG", "cc", "mi", "dInit", "dWq", "dMask", "dF2", "dYr",
    "dX0", "dX1", "dX2", "dX3", "dQb0", "dQb1", "dAg0", "dAg1",
    "dY0", "dY1", "dWp", "dOg0", "dOg1", "dWg0", "dWg1", "dDt0", "dDt1", "dDt2", "dDt3",
    "dOut0", "dOut1", "dC0", "dC1", "dC2", "dYT0", "dYT1", "dYS0", "dYS1",
    "sX", "sW", "sQK", "sQKe", "sTb", "sVe", "sXr",
    "sP", "sPd1", "sPd2", "sTr", "sPTe", "sAVd", "sNegm", "sRl", "sTrl",
    "sRlrow", "sBc", "sOTe", "sOTf", "sMask", "sLd",
    "sPP", "sPPe", "sSq", "sRstd", "sNm", "sV2", "sXo",
    "sYc", "sYt", "sYe", "sRt", "sY1",
    "sMMa", "sSg", "sGu", "sGtp", "sGte", "sDm", "sDe", "sT2", "sOut", "sY2e",
]


def build_nc():
    nc = bass.Bass("TRN2", target_bir_lowering=False)
    xT = nc.dram_tensor("xT", [C, T], F32, kind="ExternalInput").ap()
    x_own = nc.dram_tensor("x_own", [NTOK, C], F32, kind="ExternalInput").ap()
    wqkv = nc.dram_tensor("wqkv", [C, 3 * FPC], F32, kind="ExternalInput").ap()
    bqk = nc.dram_tensor("bqk", [128, 8], F32, kind="ExternalInput").ap()
    bv_row = nc.dram_tensor("bv_row", [1, FPC], F32, kind="ExternalInput").ap()
    wproj = nc.dram_tensor("wproj", [C, C], F32, kind="ExternalInput").ap()
    bp_row = nc.dram_tensor("bp_row", [1, C], F32, kind="ExternalInput").ap()
    rwT = nc.dram_tensor("rwT", [C, 8], F32, kind="ExternalInput").ap()
    wgT = nc.dram_tensor("wgT", [C, INTER], F32, kind="ExternalInput").ap()
    wuT = nc.dram_tensor("wuT", [C, INTER], F32, kind="ExternalInput").ap()
    wdT = nc.dram_tensor("wdT", [INTER, C], F32, kind="ExternalInput").ap()
    rgT = nc.dram_tensor("rgT", [C, MOE], F32, kind="ExternalInput").ap()
    ruT = nc.dram_tensor("ruT", [C, MOE], F32, kind="ExternalInput").ap()
    rdT = nc.dram_tensor("rdT", [MOE, C], F32, kind="ExternalInput").ap()
    masks = nc.dram_tensor("masks", [128, 2048], F32, kind="ExternalInput").ap()
    out = nc.dram_tensor("out", [NTOK, C], F32, kind="ExternalOutput").ap()
    KD = "ExternalOutput" if _DEBUG_TAPS else "Internal"
    q_scr = nc.dram_tensor("q_scr", [FPC, T], BF16, kind=KD).ap()
    ag_in = nc.dram_tensor("ag_in", [8 * FPC, T // 8], BF16).ap()
    ag_tap = nc.dram_tensor("ag_tap", [8 * FPC, T // 8], BF16, kind=KD).ap() if _DEBUG_TAPS else None
    ag_out = nc.dram_tensor("ag_out", [8 * C, T // 8], BF16).ap()
    y1_scr = nc.dram_tensor("y1_scr", [NTOK, C], F32, kind=KD).ap()
    y1T_scr = nc.dram_tensor("y1T_scr", [C, NTOK], BF16).ap()
    wg_bf = nc.dram_tensor("wg_bf", [C, INTER], BF16, kind=KD).ap()
    wu_bf = nc.dram_tensor("wu_bf", [C, INTER], BF16).ap()
    wd_bf = nc.dram_tensor("wd_bf", [INTER, C], BF16).ap()
    wp_bf = nc.dram_tensor("wp_bf", [C, C], BF16).ap()
    rg_bf = nc.dram_tensor("rg_bf", [C, MOE], BF16).ap()
    ru_bf = nc.dram_tensor("ru_bf", [C, MOE], BF16).ap()
    rd_bf = nc.dram_tensor("rd_bf", [MOE, C], BF16).ap()

    g = ExitStack()
    sems = {s: g.enter_context(nc.semaphore(s)) for s in SEM_NAMES}
    ident_bf = g.enter_context(nc.sbuf_tensor("ident_bf", [128, 128], BF16))
    ident_32 = g.enter_context(nc.sbuf_tensor("ident_32", [128, 128], F32))
    ones32 = g.enter_context(nc.sbuf_tensor("ones32", [1, 128], F32))
    iot = g.enter_context(nc.sbuf_tensor("iot", [128, 128], I32))
    bqk_sb = g.enter_context(nc.sbuf_tensor("bqk_sb", [128, 8], F32))
    bvb = g.enter_context(nc.sbuf_tensor("bvb", [128, FPC], F32))
    comb_sb = g.enter_context(nc.sbuf_tensor("comb_sb", [128, 16], F32))
    cstab_sb = g.enter_context(nc.sbuf_tensor("cstab_sb", [128, 1], F32))
    ones_bf = g.enter_context(nc.sbuf_tensor("ones_bf", [128, 1], BF16))

    P = Plan()

    s1 = ExitStack()
    K_sb = s1.enter_context(nc.sbuf_tensor("K_sb", [128, HG * T], BF16))
    V_sb = s1.enter_context(nc.sbuf_tensor("V_sb", [128, (T // 128) * FPC], BF16))
    s1a = ExitStack()
    wqkv_sb = s1a.enter_context(nc.sbuf_tensor("wqkv_sb", [128, 16 * 3 * FPC], BF16))
    xbf = s1a.enter_context(nc.sbuf_tensor("xbf", [128, 8192], BF16))
    xstage = s1a.enter_context(nc.sbuf_tensor("xstage", [128, 2 * 512], F32))
    qbuf = s1a.enter_context(nc.sbuf_tensor("qbuf", [128, 2 * 512], BF16))
    bvr = s1a.enter_context(nc.sbuf_tensor("bvr", [1, FPC], F32))
    ps1 = ExitStack()
    pqk = [ps1.enter_context(nc.psum_tensor(f"pqk{i}", [128, 512], F32)) for i in range(8)]

    P.emit("sync", lambda e, s: e.dma_start(out=bqk_sb[:], in_=bqk), inc=("dInit", 16))
    P.emit("sync", lambda e, s: e.dma_start(out=bvr[:], in_=bv_row), inc=("dInit", 16))
    P.emit("gpsimd", lambda e, s: e.iota(iot[:], pattern=[[1, 128]], base=0,
                                         channel_multiplier=-1), inc=("mi", 1))
    P.emit("vector", lambda e, s: e.memset(ones32[:], 1.0))
    P.emit("vector", lambda e, s: e.memset(cstab_sb[:], -14.0))
    P.emit("vector", lambda e, s: e.memset(ones_bf[:], 1.0))
    P.emit("vector", lambda e, s: e.tensor_scalar(ident_bf[:], iot[:], 0.0, None, OP.is_equal),
           waits=[("mi", 1)])
    P.emit("vector", lambda e, s: e.tensor_scalar(ident_32[:], iot[:], 0.0, None, OP.is_equal))
    P.emit("vector", lambda e, s: e.drain(), inc=("mi", 1))
    P.emit("tensor", lambda e, s: e.matmul(pqk[0][:], ones32[0:1, :], bvr[0:1, 0:512],
                                           start=True, stop=True),
           waits=[("mi", 2), ("dInit", 32)], inc=("mi", 1))
    P.emit("vector", lambda e, s: e.tensor_copy(bvb[:, 0:512], pqk[0][:]),
           waits=[("mi", 3)])
    P.emit("vector", lambda e, s: e.drain(), inc=("mi", 1))

    for hq in range(32):
        hc, half = divmod(hq, 2)
        P.emit("sync", (lambda hc, half: lambda e, s: e.dma_start(
            out=xstage[:, 0:768], in_=wqkv[hc * 128:(hc + 1) * 128,
                                           half * 768:(half + 1) * 768]))(hc, half),
            waits=[("sW", P.val("sW")), ("mi", 4) if hq == 0 else ("mi", 0)], inc=("dWq", 16))
        P.emit("vector", (lambda hc, half: lambda e, s: e.tensor_copy(
            wqkv_sb[:, hc * 1536 + half * 768: hc * 1536 + (half + 1) * 768],
            xstage[:, 0:768]))(hc, half),
            waits=[("dWq", P.val("dWq"))], inc=("sW", 1))

    qdma = []
    for tb in range(16):
        xoff = 0
        for hc in range(16):
            slot = hc % 2
            n = tb * 16 + hc
            P.emit("sync", (lambda hc, slot, tb: lambda e, s: e.dma_start(
                out=xstage[:, slot * 512:(slot + 1) * 512],
                in_=xT[hc * 128:(hc + 1) * 128, tb * 512:(tb + 1) * 512]))(hc, slot, tb),
                waits=[("sX", n - 1)] + ([("sW", 32)] if n < 2 else []),
                inc=(f"dX{slot}", 16))
            w = [(f"dX{slot}", P.val(f"dX{slot}"))]
            if tb >= 1:
                w.append(("sXr", 16 * (tb - 1) + hc + 1))
            P.emit("vector", (lambda hc, slot, xoff: lambda e, s: e.tensor_copy(
                xbf[:, xoff + hc * 512: xoff + (hc + 1) * 512],
                xstage[:, slot * 512:(slot + 1) * 512]))(hc, slot, xoff),
                waits=w, inc=("sX", 1))
        for hc in range(16):
            for ft in range(8):
                col0 = (ft % 4) * 128 + (ft // 4) * 512
                w = []
                if ft == 0:
                    w.append(("sX", 16 * tb + hc + 1))
                if hc == 0:
                    if ft == 0:
                        w.append(("mi", 4))
                        w.append(("sW", 32))
                    if tb >= 1:
                        w.append(("sQKe", 8 * (tb - 1) + ft + 1))
                        if ft < 4:
                            w.append(("sVe", 4 * (tb - 1) + ft + 1))
                P.emit("tensor", (lambda hc, ft, col0, tb: lambda e, s: e.matmul(
                    pqk[ft][:], wqkv_sb[:, hc * 1536 + col0: hc * 1536 + col0 + 128],
                    xbf[:, hc * 512:(hc + 1) * 512],
                    start=(hc == 0), stop=(hc == 15)))(hc, ft, col0, tb),
                    waits=w, inc=("sQK", 1) if hc == 15 else None)
        for ft in range(8):
            gq = P.val("sQK") - 7 + ft
            if ft < 4:
                qb = (tb * 4 + ft) % 2
                wts = [("sQK", gq), ("dInit", 32)]
                if len(qdma) >= 2:
                    wts.append((f"dQb{qb}", qdma[-2][1]))
                P.emit("vector", (lambda ft, qb: lambda e, s: e.tensor_scalar(
                    qbuf[:, qb * 512:(qb + 1) * 512], pqk[ft][:],
                    bqk_sb[:, ft:ft + 1], RSQ, OP.add, OP.mult))(ft, qb),
                    waits=wts, inc=("sQKe", 1))
                P.emit("sync", (lambda ft, qb, tb: lambda e, s: e.dma_start(
                    out=q_scr[ft * 128:(ft + 1) * 128, tb * 512:(tb + 1) * 512],
                    in_=qbuf[:, qb * 512:(qb + 1) * 512]))(ft, qb, tb),
                    waits=[("sQKe", P.val("sQKe"))], inc=(f"dQb{qb}", 16))
                qdma.append((qb, P.val(f"dQb{qb}")))
            else:
                P.emit("vector", (lambda ft, tb: lambda e, s: e.tensor_scalar(
                    K_sb[:, (ft - 4) * T + tb * 512: (ft - 4) * T + (tb + 1) * 512],
                    pqk[ft][:], bqk_sb[:, ft:ft + 1], None, OP.add))(ft, tb),
                    waits=[("sQK", gq), ("dInit", 32)], inc=("sQKe", 1))
        for hc in range(16):
            for tt in range(4):
                w = []
                if hc == 0:
                    w.append(("sQKe", 8 * tb + tt + 1))
                P.emit("tensor", (lambda hc, tt, tb: lambda e, s: e.matmul(
                    pqk[tt][:], xbf[:, hc * 512 + tt * 128: hc * 512 + tt * 128 + 128],
                    wqkv_sb[:, hc * 1536 + 1024: hc * 1536 + 1536],
                    start=(hc == 0), stop=(hc == 15)))(hc, tt, tb),
                    waits=w, inc=("sXr", 1) if tt == 3 else None)
        for tt in range(4):
            P.emit("vector", (lambda tt, tb: lambda e, s: e.tensor_tensor(
                V_sb[:, (tb * 4 + tt) * FPC:(tb * 4 + tt + 1) * FPC], pqk[tt][:],
                bvb[:, 0:512], OP.add))(tt, tb),
                waits=[("sXr", 16 * (tb + 1)), ("mi", 4)], inc=("sVe", 1))
    P.emit("sync", lambda e, s: e.nop(),
           waits=[("dQb0", P.val("dQb0")), ("dQb1", P.val("dQb1")),
                  ("sVe", P.val("sVe")), ("sQKe", P.val("sQKe"))])
    P.check_deadlock("block")
    with nc.Block() as blk:
        P.replay(blk, sems)
    ps1.close()
    s1a.close()

    s1b = ExitStack()
    masks_sb = s1b.enter_context(nc.sbuf_tensor("masks_sb", [128, 2048], F32))
    Qs = s1b.enter_context(nc.sbuf_tensor("Qs", [128, 3 * 512], BF16))
    PT_sb = s1b.enter_context(nc.sbuf_tensor("PT_sb", [128, 3 * 2048], BF16))
    OTbf = s1b.enter_context(nc.sbuf_tensor("OTbf", [128, 2 * 512], BF16))
    mparts = s1b.enter_context(nc.sbuf_tensor("mparts", [128, 64], F32))
    OTf = s1b.enter_context(nc.sbuf_tensor("OTf", [128, 512], F32))
    lrow = s1b.enter_context(nc.sbuf_tensor("lrow", [1, 512], F32))
    l4 = s1b.enter_context(nc.sbuf_tensor("l4", [128, 4], F32))
    rl4 = s1b.enter_context(nc.sbuf_tensor("rl4", [128, 4], F32))
    rlrow = s1b.enter_context(nc.sbuf_tensor("rlrow", [1, 512], F32))
    rlb = s1b.enter_context(nc.sbuf_tensor("rlb", [128, 512], F32))
    wstage_f = s1b.enter_context(nc.sbuf_tensor("wstage_f", [128, 4096], F32))
    wstage_b = s1b.enter_context(nc.sbuf_tensor("wstage_b", [128, 4096], BF16))
    ps2 = ExitStack()
    pP = [ps2.enter_context(nc.psum_tensor(f"pP{i}", [128, 512], F32)) for i in range(6)]
    pOT = ps2.enter_context(nc.psum_tensor("pOT", [128, 512], F32))
    pL = ps2.enter_context(nc.psum_tensor("pL", [1, 512], F32))

    P.emit("sync", lambda e, s: e.dma_start(out=masks_sb[:], in_=masks), inc=("dMask", 16))

    for (src, dst, R_, C_) in ((wproj, wp_bf, C, C), (wgT, wg_bf, C, INTER), (wuT, wu_bf, C, INTER),
                               (wdT, wd_bf, INTER, C), (rgT, rg_bf, C, MOE),
                               (ruT, ru_bf, C, MOE), (rdT, rd_bf, MOE, C)):
        width = min(C_, 4096)
        for r in range(R_ // 128):
            for cb in range(max(1, C_ // width)):
                dv = P.val("dG") + 16
                P.emit("gpsimd", (lambda src, r, cb, width: lambda e, s: e.dma_start(
                    out=wstage_f[:, 0:width],
                    in_=src[r * 128:(r + 1) * 128, cb * width:(cb + 1) * width]))(src, r, cb, width),
                    inc=("dG", 16))
                P.emit("gpsimd", (lambda width: lambda e, s: e.tensor_copy(
                    wstage_b[:, 0:width], wstage_f[:, 0:width]))(width),
                    waits=[("dG", dv)])
                P.emit("gpsimd", lambda e, s: e.drain())
                P.emit("gpsimd", (lambda dst, r, cb, width: lambda e, s: e.dma_start(
                    out=dst[r * 128:(r + 1) * 128, cb * width:(cb + 1) * width],
                    in_=wstage_b[:, 0:width]))(dst, r, cb, width),
                    inc=("dG", 16))
    dG_wcast = P.val("dG")

    sPgrp_after_p2 = {}
    nP = 0
    kbg = 0
    agdma = []
    consumers = []
    CSTAB = -14.0

    def q_strip_dma(itn):
        j, h = ITS[itn]
        w = []
        if itn >= 3:
            w.append(("sP", sPgrp_after_p2[itn - 3]))
        P.emit("scalar", (lambda j, h, itn: lambda e, s: e.dma_start(
            out=Qs[:, (itn % 3) * 512:(itn % 3 + 1) * 512],
            in_=q_scr[h * 128:(h + 1) * 128, j * 512:(j + 1) * 512]))(j, h, itn),
            waits=w, inc=(f"dC{itn % 3}", 16))

    JS = [0, 1, 4, 5, 8, 9, 12, 13, 2, 3, 6, 7, 10, 11, 14, 15]
    ITS = [(j, h) for j in JS for h in range(4)]
    ag_counts = []
    q_strip_dma(0)
    q_strip_dma(1)

    def p2_mm(it, kb, j, h):
        nonlocal nP
        banks = []
        base = P.val("sPd2")
        for kt in range(4):
            bank = pP[nP % 6]
            banks.append(bank)
            w = []
            if nP >= 6:
                w.append(consumers[nP - 6])
            if kb == 0 and kt == 0:
                w.append((f"dC{it % 3}", 16 * (it // 3 + 1)))
                if it == 0:
                    w.append(("dMask", 16))
            P.emit("tensor", (lambda bank, kt, h, kb, it: lambda e, s: e.matmul(
                bank[:], K_sb[:, h * T + kb * 512 + kt * 128: h * T + kb * 512 + kt * 128 + 128],
                Qs[:, (it % 3) * 512:(it % 3 + 1) * 512],
                start=True, stop=True))(bank, kt, h, kb, it),
                waits=w, inc=("sP", 1))
            consumers.append(("sPd2", base + kt + 1))
            nP += 1
        if kb == j:
            sPv0 = P.val("sP") - 4
            for kt in range(4):
                P.emit("vector", (lambda bank, kt: lambda e, s: e.tensor_tensor(
                    bank[:], bank[:], masks_sb[:, kt * 512:(kt + 1) * 512],
                    OP.add))(banks[kt], kt),
                    waits=[("sP", sPv0 + kt + 1)],
                    inc=("sMask", 1))
        return banks

    def p2_soft(it, kb, banks, sPv0, j, h):
        nonlocal kbg
        for kt in range(4):
            w = [("sP", sPv0 + kt + 1)]
            if kb == j:
                w.append(("sMask", P.val("sMask") - 3 + kt))
            if kt == 0:
                if kbg >= 3:
                    w.append(("sAVd", kbg - 2))
            P.emit("scalar", (lambda bank, kt, kbg: lambda e, s: e.activation(
                PT_sb[:, (kbg % 3) * 2048 + kt * 512:(kbg % 3) * 2048 + (kt + 1) * 512],
                bank[:], AF.Exp, bias=cstab_sb[:], scale=1.0))(banks[kt], kt, kbg),
                waits=w, inc=("sPd2", 1))
        sPdv = P.val("sPd2")
        for kt in range(4):
            ttg = kb * 4 + kt
            w = [("sPd2", sPdv - 3 + kt)]
            if kt == 0 and kb == 0:
                if it >= 1:
                    w.append(("sOTe", it))
                    w.append(("sRlrow", it))
            P.emit("tensor", (lambda kt, ttg, h, kb, j, kbg: lambda e, s: e.matmul(
                pOT[:], V_sb[:, ttg * FPC + h * 128: ttg * FPC + (h + 1) * 128],
                PT_sb[:, (kbg % 3) * 2048 + kt * 512:(kbg % 3) * 2048 + (kt + 1) * 512],
                start=(kb == 0 and kt == 0), stop=(kb == j and kt == 3),
                skip_group_check=True))(kt, ttg, h, kb, j, kbg),
                waits=w, inc=("sAVd", 1) if kt == 3 else None)
            P.emit("tensor", (lambda kt, kb, j, kbg: lambda e, s: e.matmul(
                pL[:], ones_bf[:, 0:1],
                PT_sb[:, (kbg % 3) * 2048 + kt * 512:(kbg % 3) * 2048 + (kt + 1) * 512],
                start=(kb == 0 and kt == 0), stop=(kb == j and kt == 3),
                skip_group_check=True))(kt, kb, j, kbg),
                inc=("sLd", 1) if (kb == j and kt == 3) else None)
        kbg += 1

    def p2_tail(it, j, h):
        sAVv = P.val("sAVd")
        P.emit("vector", lambda e, s: e.tensor_copy(OTf[:], pOT[:]),
               waits=[("sAVd", sAVv)])
        P.emit("vector", lambda e, s: e.drain(), inc=("sOTf", 1))
        P.emit("vector", lambda e, s: e.tensor_copy(lrow[:], pL[:]),
               waits=[("sLd", it + 1)])
        P.emit("vector", lambda e, s: e.drain())
        P.emit("vector", lambda e, s: e.reciprocal(rlrow[:], lrow[:]))
        P.emit("vector", lambda e, s: e.drain(), inc=("sRlrow", 1))
        P.emit("tensor", lambda e, s: e.matmul(pOT[:], ones32[0:1, :], rlrow[0:1, :],
                                               start=True, stop=True),
               waits=[("sRlrow", it + 1), ("sOTf", it + 1)], inc=("sBc", 1))
        P.emit("vector", lambda e, s: e.tensor_copy(rlb[:], pOT[:]),
               waits=[("sBc", P.val("sBc"))])
        P.emit("vector", lambda e, s: e.drain())
        w = []
        if it >= 2:
            w.append((f"dAg{it % 2}", agdma[it - 2]))
        P.emit("vector", (lambda it: lambda e, s: e.tensor_tensor(
            OTbf[:, (it % 2) * 512:(it % 2 + 1) * 512], OTf[:], rlb[:],
            OP.mult))(it),
            waits=w, inc=("sOTe", 1))
        P.emit("sync", (lambda it, j, h: lambda e, s: e.dma_start(
            out=ag_in[(j // 2) * 512 + h * 128:(j // 2) * 512 + (h + 1) * 128,
                      (j % 2) * 512:(j % 2 + 1) * 512],
            in_=OTbf[:, (it % 2) * 512:(it % 2 + 1) * 512]))(it, j, h),
            waits=[("sOTe", it + 1)], inc=(f"dAg{it % 2}", 16))
        agdma.append(P.val(f"dAg{it % 2}"))

    for it in range(64):
        j, h = ITS[it]
        for k in range(j + 1):
            sPv0 = P.val("sP")
            banks2 = p2_mm(it, k, j, h)
            p2_soft(it, k, banks2, sPv0, j, h)
        sPgrp_after_p2[it] = P.val("sP")
        p2_tail(it, j, h)
        ag_counts.append((P.val("dAg0"), P.val("dAg1")))
        if it < 62:
            q_strip_dma(it + 2)

    def emit_ag(bq):
        last_idx = max(i for i, (jx, hx) in enumerate(ITS) if jx // 2 == bq)
        c0, c1 = ag_counts[last_idx]
        P.emit("gpsimd", (lambda bq: lambda e, s: e.collective_compute(
            "AllGather", OP.bypass,
            replica_groups=[[0, 1, 2, 3], [4, 5, 6, 7]],
            ins=[ag_in[bq * 512:(bq + 1) * 512, :].opt()],
            outs=[ag_out[bq * 2048:(bq + 1) * 2048, :].opt()]))(bq),
            waits=[("dAg0", c0), ("dAg1", c1)], inc=("cc", 1))

    for bq in (0, 2, 4, 6):
        emit_ag(bq)
    P.emit("sync", lambda e, s: e.nop(), waits=[("dAg0", 32 * 16), ("dAg1", 32 * 16),
                                                ("dMask", 16)])
    P.emit("scalar", lambda e, s: e.nop(), waits=[("dC0", P.val("dC0")), ("dC1", P.val("dC1"))])
    P.check_deadlock("block")
    with nc.Block() as blk:
        P.replay(blk, sems)
    ps2.close()
    s1b.close()
    s1.close()

    s2a = ExitStack()
    y1TS = s2a.enter_context(nc.sbuf_tensor("y1TS", [128, 2 * 2048], BF16))
    wp_sb = s2a.enter_context(nc.sbuf_tensor("wp_sb", [128, 16 * 2048], BF16))
    OTg = s2a.enter_context(nc.sbuf_tensor("OTg", [128, 2 * 2048], BF16))
    xpb = s2a.enter_context(nc.sbuf_tensor("xpb", [128, 2048], F32))
    bpb = s2a.enter_context(nc.sbuf_tensor("bpb", [128, 2048], F32))
    y1t = s2a.enter_context(nc.sbuf_tensor("y1t", [128, 2 * 2048], F32))
    y1b = s2a.enter_context(nc.sbuf_tensor("y1b", [128, 2048], BF16))
    junk2 = s2a.enter_context(nc.sbuf_tensor("junk2", [128, 2048], BF16))
    fstage2 = s2a.enter_context(nc.sbuf_tensor("fstage2", [128, 2048], F32))
    bpr = s2a.enter_context(nc.sbuf_tensor("bpr", [1, 2048], F32))
    rwbf = s2a.enter_context(nc.sbuf_tensor("rwbf", [128, 128], BF16))
    eq8 = s2a.enter_context(nc.sbuf_tensor("eq8", [128, 8], F32))
    msk8 = s2a.enter_context(nc.sbuf_tensor("msk8", [128, 8], F32))
    sc1 = s2a.enter_context(nc.sbuf_tensor("sc1", [128, 8], F32))
    vs1 = s2a.enter_context(nc.sbuf_tensor("vs1", [128, 2], F32))
    ps3 = ExitStack()
    pp = [ps3.enter_context(nc.psum_tensor(f"pp{i}", [128, 512], F32)) for i in range(2)]
    prt = ps3.enter_context(nc.psum_tensor("prt", [128, 8], F32))
    pyt = ps3.enter_context(nc.psum_tensor("pyt", [128, 2048], BF16))

    dynoff = {}

    def setup_dynoff(e, s):
        reg = e.register("r_qr").__enter__()
        e.reg_mov(reg, e.partition_id())
        e.reg_alu(reg, reg, 3, OP.bitwise_and)
        e.reg_alu(reg, reg, 2, OP.mult)
        dynoff[0] = e.snap(reg)
        e.reg_alu(reg, reg, 1, OP.add)
        dynoff[1] = e.snap(reg)
        return None
    P.ops["sync"].append(lambda e, s: setup_dynoff(e, s))

    P.emit("sync", lambda e, s: e.dma_start(out=bpr[:], in_=bp_row), inc=("dInit", 16))
    for fb in range(4):
        w = [("dInit", 48)] if fb == 0 else []
        if fb >= 2:
            w.append(("sPPe", fb - 1))
        P.emit("tensor", (lambda fb: lambda e, s: e.matmul(
            pp[fb % 2][:], ones32[0:1, :], bpr[0:1, fb * 512:(fb + 1) * 512],
            start=True, stop=True))(fb),
            waits=w, inc=("sPP", 1))
        P.emit("vector", (lambda fb: lambda e, s: e.tensor_copy(
            bpb[:, fb * 512:(fb + 1) * 512], pp[fb % 2][:]))(fb),
            waits=[("sPP", P.val("sPP"))], inc=("sPPe", 1))
    for bq in (1, 3, 5, 7):
        emit_ag(bq)
    P.emit("sync", lambda e, s: e.dma_start(
        out=wp_sb[:].rearrange("p (c f) -> p c f", c=16),
        in_=wp_bf.rearrange("(c p) f -> p c f", p=128)),
        waits=[("dG", 16 * 16 * 2)], inc=("dWp", 16))
    P.emit("sync", lambda e, s: e.dma_start(
        out=fstage2[:, 0:128].rearrange("p (c e) -> p c e", c=16),
        in_=rwT.rearrange("(c p) e -> p c e", p=128)),
        inc=("dF2", 16))
    P.emit("vector", lambda e, s: e.tensor_copy(rwbf[:], fstage2[:, 0:128]),
           waits=[("dF2", P.val("dF2"))], inc=("sW", 1))

    y1dma = []
    for tt in range(16):
        w = [("cc", 4)] if tt == 0 else ([("cc", 8)] if tt == 8 else [])
        if tt >= 2:
            w.append(("sPP", 4 + 4 * (tt - 1)))
        P.emit("sync", (lambda tt: lambda e, s: e.dma_start(
            out=OTg[:, (tt % 2) * 2048:(tt % 2 + 1) * 2048].rearrange(
                "p (q c t) -> p q c t", q=1, c=16),
            in_=ag_out.rearrange("(q c p) t -> p q c t", p=128, c=16)[
                :, bass.ds(dynoff[tt // 8], 1), :, (tt % 8) * 128:(tt % 8 + 1) * 128]))(tt),
            waits=w, inc=(f"dOg{tt % 2}", 16))
        dv_otg = (f"dOg{tt % 2}", P.val(f"dOg{tt % 2}"))
        P.emit("sync", (lambda tt: lambda e, s: e.dma_start(
            out=fstage2[:], in_=x_own[tt * 128:(tt + 1) * 128, :]))(tt),
            waits=[("sXo", tt), ("sW", 33)], inc=("dF2", 16))
        w = [("dF2", P.val("dF2")), ("sPPe", 4)]
        if tt >= 1:
            w.append(("sPPe", 4 + 4 * tt))
        P.emit("vector", lambda e, s: e.tensor_tensor(xpb[:], fstage2[:], bpb[:], OP.add),
               waits=w)
        P.emit("vector", lambda e, s: e.drain(), inc=("sXo", 1))
        for fb in range(4):
            gp = P.val("sPP")
            bank = pp[gp % 2]
            for dc in range(16):
                w = []
                if dc == 0:
                    w.append(dv_otg)
                    if gp >= 6:
                        w.append(("sPPe", gp - 1))
                P.emit("tensor", (lambda dc, fb, tt, bank: lambda e, s: e.matmul(
                    bank[:], OTg[:, (tt % 2) * 2048 + dc * 128:(tt % 2) * 2048 + dc * 128 + 128],
                    wp_sb[:, dc * 2048 + fb * 512: dc * 2048 + fb * 512 + 512],
                    start=(dc == 0), stop=(dc == 15)))(dc, fb, tt, bank),
                    waits=w, inc=("sPP", 1) if dc == 15 else None)
            w = [("sPP", P.val("sPP")), ("sXo", tt + 1)]
            if fb == 0:
                if tt >= 2:
                    w.append((f"dY{tt % 2}", y1dma[tt - 2]))
                if tt >= 1:
                    w.append(("sYc", tt))
            P.emit("vector", (lambda fb, tt, bank: lambda e, s: e.tensor_tensor(
                y1t[:, (tt % 2) * 2048 + fb * 512:(tt % 2) * 2048 + (fb + 1) * 512],
                bank[:], xpb[:, fb * 512:(fb + 1) * 512], OP.add))(fb, tt, bank),
                waits=w, inc=("sPPe", 1))
        y1v = y1t[:, (tt % 2) * 2048:(tt % 2 + 1) * 2048]
        P.emit("vector", lambda e, s: e.drain())
        P.emit("vector", (lambda y1v: lambda e, s: e.reduce_sum(
            sc1[:, 6:7], y1v, axis=AX.X, negate=True))(y1v))
        P.emit("vector", lambda e, s: e.drain())
        P.emit("vector", lambda e, s: e.tensor_scalar(sc1[:, 6:7], sc1[:, 6:7],
                                                      1.0 / 2048.0, None, OP.mult))
        P.emit("vector", lambda e, s: e.drain(), inc=("sNm", 1))
        P.emit("scalar", (lambda y1v: lambda e, s: e.activation(
            junk2[:], y1v, AF.Square, bias=sc1[:, 6:7], scale=1.0,
            accum_out=vs1[:, 0:1]))(y1v),
            waits=[("sNm", tt + 1)], inc=("sSq", 1))
        P.emit("vector", lambda e, s: e.tensor_scalar(vs1[:, 1:2], vs1[:, 0:1],
                                                      1.0 / 2048.0, EPS, OP.mult, OP.add),
               waits=[("sSq", P.val("sSq"))])
        P.emit("vector", lambda e, s: e.drain(), inc=("sV2", 1))
        P.emit("scalar", lambda e, s: e.activation(sc1[:, 5:6], vs1[:, 1:2], AF.Ln),
               waits=[("sV2", P.val("sV2"))])
        P.emit("scalar", lambda e, s: e.drain())
        P.emit("scalar", lambda e, s: e.activation(sc1[:, 4:5], sc1[:, 5:6], AF.Exp,
                                                   scale=-0.5))
        P.emit("scalar", lambda e, s: e.drain(), inc=("sRstd", 1))
        P.emit("vector", (lambda y1v: lambda e, s: e.tensor_scalar(
            y1v, y1v, sc1[:, 6:7], sc1[:, 4:5], OP.add, OP.mult))(y1v),
            waits=[("sRstd", P.val("sRstd"))])
        P.emit("vector", lambda e, s: e.drain(), inc=("sY1", 1))
        P.emit("sync", (lambda tt, y1v: lambda e, s: e.dma_start(
            out=y1_scr[tt * 128:(tt + 1) * 128, :], in_=y1v))(tt, y1v),
            waits=[("sY1", tt + 1)], inc=(f"dY{tt % 2}", 16))
        y1dma.append(P.val(f"dY{tt % 2}"))
        w = [("sYt", tt)] if tt >= 1 else []
        P.emit("vector", (lambda y1v: lambda e, s: e.tensor_copy(y1b[:], y1v))(y1v), waits=w)
        P.emit("vector", lambda e, s: e.drain(), inc=("sYc", tt + 1 - P.val("sYc")))
        for hc in range(16):
            w = [("sYc", tt + 1)] if hc == 0 else []
            if hc == 0 and tt >= 1:
                w.append(("sYe", tt))
            P.emit("tensor", (lambda hc: lambda e, s: e.transpose(
                pyt[:, hc * 128:(hc + 1) * 128], y1b[:, hc * 128:(hc + 1) * 128],
                ident_bf[:]))(hc),
                waits=w, inc=("sYt", 1) if hc == 15 else None)
        w = [("sYt", tt + 1)]
        if tt >= 2:
            w.append((f"dYT{tt % 2}", 16 * (tt // 2)))
            w.append(("sRt", tt - 1))
        P.emit("vector", (lambda tt: lambda e, s: e.tensor_copy(
            y1TS[:, (tt % 2) * 2048:(tt % 2 + 1) * 2048], pyt[:]))(tt),
            waits=w, inc=("sYe", 1))
        P.emit("sync", (lambda tt: lambda e, s: e.dma_start(
            out=y1T_scr.rearrange("(c p) t -> p c t", p=128)[:, :, tt * 128:(tt + 1) * 128],
            in_=y1TS[:, (tt % 2) * 2048:(tt % 2 + 1) * 2048].rearrange("p (c n) -> p c n", c=16)))(tt),
            waits=[("sYe", tt + 1)], inc=(f"dYT{tt % 2}", 16))
        for hc in range(16):
            w = [("sYe", tt + 1)] if hc == 0 else []
            if hc == 0 and tt >= 1:
                w.append(("sRt", tt))
            P.emit("tensor", (lambda hc, tt: lambda e, s: e.matmul(
                prt[:], y1TS[:, (tt % 2) * 2048 + hc * 128:(tt % 2) * 2048 + hc * 128 + 128],
                rwbf[:, hc * 8:(hc + 1) * 8],
                start=(hc == 0), stop=(hc == 15)))(hc, tt),
                waits=w, inc=("sBc", 1) if hc == 15 else None)
        P.emit("vector", lambda e, s: e.reduce_max(sc1[:, 0:1], prt[:], axis=AX.X),
               waits=[("sBc", P.val("sBc"))])
        P.emit("vector", lambda e, s: e.drain())
        P.emit("vector", lambda e, s: e.tensor_scalar(eq8[:], prt[:], sc1[:, 0:1], None, OP.is_ge))
        P.emit("vector", lambda e, s: e.drain())
        P.emit("vector", lambda e, s: e.scalar_tensor_tensor(
            msk8[:], eq8[:], NEG, prt[:], OP.mult, OP.add))
        P.emit("vector", lambda e, s: e.drain(), inc=("sRt", 1))
        P.emit("vector", lambda e, s: e.reduce_max(sc1[:, 1:2], msk8[:], axis=AX.X))
        P.emit("vector", lambda e, s: e.drain())
        P.emit("vector", lambda e, s: e.tensor_scalar(sc1[:, 1:2], sc1[:, 1:2],
                                                      sc1[:, 0:1], None, OP.subtract))
        P.emit("vector", lambda e, s: e.drain(), inc=("sV2", 1))
        P.emit("scalar", lambda e, s: e.activation(sc1[:, 2:3], sc1[:, 1:2], AF.Exp),
               waits=[("sV2", P.val("sV2"))], inc=("sSq", 1))
        P.emit("vector", lambda e, s: e.tensor_scalar(sc1[:, 3:4], sc1[:, 2:3], 1.0,
                                                      None, OP.add),
               waits=[("sSq", P.val("sSq"))])
        P.emit("vector", lambda e, s: e.drain())
        P.emit("vector", lambda e, s: e.reciprocal(sc1[:, 7:8], sc1[:, 3:4]))
        P.emit("vector", lambda e, s: e.drain())
        P.emit("vector", (lambda tt: lambda e, s: e.scalar_tensor_tensor(
            comb_sb[:, tt:tt + 1], sc1[:, 2:3], sc1[:, 7:8], sc1[:, 7:8],
            OP.mult, OP.add))(tt))
        P.emit("vector", lambda e, s: e.drain())
    P.emit("sync", lambda e, s: e.nop(),
           waits=[("dY0", P.val("dY0")), ("dY1", P.val("dY1")),
                  ("dOg0", P.val("dOg0")), ("dOg1", P.val("dOg1")),
                  ("dF2", P.val("dF2")), ("dYT0", P.val("dYT0")),
                  ("dYT1", P.val("dYT1"))])
    P.check_deadlock("block")
    with nc.Block() as blk:
        P.replay(blk, sems)
    ps3.close()
    s2a.close()

    s2b = ExitStack()
    guT_sb = s2b.enter_context(nc.sbuf_tensor("guT_sb", [128, 64 * 512], BF16))
    guTr_sb = s2b.enter_context(nc.sbuf_tensor("guTr_sb", [128, 8 * 512], BF16))
    y1ts = s2b.enter_context(nc.sbuf_tensor("y1ts", [128, 2 * 16 * 512], BF16))
    wgt_i = s2b.enter_context(nc.sbuf_tensor("wgt_i", [128, 2 * 2048], BF16))
    wut_i = s2b.enter_context(nc.sbuf_tensor("wut_i", [128, 2 * 2048], BF16))
    wdt = s2b.enter_context(nc.sbuf_tensor("wdt", [128, 4 * 512], BF16))
    dsh = s2b.enter_context(nc.sbuf_tensor("dsh", [128, 4 * 2048], F32))
    sgv = s2b.enter_context(nc.sbuf_tensor("sgv", [128, 2 * 512], BF16))
    y1rb = s2b.enter_context(nc.sbuf_tensor("y1rb", [128, 2048], F32))
    y2 = s2b.enter_context(nc.sbuf_tensor("y2", [128, 2 * 2048], F32))
    ln2s = s2b.enter_context(nc.sbuf_tensor("ln2s", [128, 4], F32))
    junk3 = s2b.enter_context(nc.sbuf_tensor("junk3", [128, 2048], BF16))
    ps4 = ExitStack()
    pgA = [ps4.enter_context(nc.psum_tensor(f"pgA{i}", [128, 512], F32)) for i in range(2)]
    puA = [ps4.enter_context(nc.psum_tensor(f"puA{i}", [128, 512], F32)) for i in range(2)]
    pd = [ps4.enter_context(nc.psum_tensor(f"pd{i}", [128, 512], F32)) for i in range(4)]

    icA = 0
    icg = 0
    mma_after_tb = []
    icg_after_tb = 0
    for tbF in range(4):
        w = []
        if tbF >= 2:
            w.append(("sMMa", mma_after_tb[tbF - 2]))
        P.emit("sync", (lambda tbF: lambda e, s: e.dma_start(
            out=y1ts[:, (tbF % 2) * 8192:(tbF % 2 + 1) * 8192].rearrange("p (c t) -> p c t", c=16),
            in_=y1T_scr.rearrange("(c p) t -> p c t", p=128)[:, :, tbF * 512:(tbF + 1) * 512]))(tbF),
            waits=w + [("dYT0", 8 * 16), ("dYT1", 8 * 16)], inc=(f"dYS{tbF % 2}", 16))
        dv_ys = (f"dYS{tbF % 2}", P.val(f"dYS{tbF % 2}"))
        for routed in (0, 1):
            nic = (MOE if routed else INTER) // 128
            wg_src, wu_src = (rg_bf, ru_bf) if routed else (wg_bf, wu_bf)
            gdst = guTr_sb if routed else guT_sb
            for ic in range(nic):
                slot = icA % 2
                w = [("dG", dG_wcast)] if icA == 0 else []
                if icA >= 2:
                    w.append(("sMMa", 2 * (icA - 2) + 2))
                P.emit("sync", (lambda wg_src, ic, slot: lambda e, s: e.dma_start(
                    out=wgt_i[:, slot * 2048:(slot + 1) * 2048].rearrange("p (c i) -> p c i", c=16),
                    in_=wg_src.rearrange("(c p) i -> p c i", p=128)[:, :, ic * 128:(ic + 1) * 128]))(wg_src, ic, slot),
                    waits=w, inc=(f"dWg{slot}", 16))
                P.emit("sync", (lambda wu_src, ic, slot: lambda e, s: e.dma_start(
                    out=wut_i[:, slot * 2048:(slot + 1) * 2048].rearrange("p (c i) -> p c i", c=16),
                    in_=wu_src.rearrange("(c p) i -> p c i", p=128)[:, :, ic * 128:(ic + 1) * 128]))(wu_src, ic, slot),
                    inc=(f"dWg{slot}", 16))
                dv_w = (f"dWg{slot}", P.val(f"dWg{slot}"))
                for gu_i, (wsb, bankp) in enumerate(((wgt_i, pgA), (wut_i, puA))):
                    bank = bankp[icA % 2]
                    for hc in range(16):
                        w = []
                        if hc == 0:
                            w.append(dv_w)
                            w.append(dv_ys)
                            if gu_i == 0 and icA >= 2:
                                w.append(("sSg", icA - 1))
                            if gu_i == 1 and icA >= 2:
                                w.append(("sGu", icA - 1))
                        P.emit("tensor", (lambda hc, bank, wsb, slot, tbF: lambda e, s: e.matmul(
                            bank[:], wsb[:, slot * 2048 + hc * 128: slot * 2048 + hc * 128 + 128],
                            y1ts[:, (tbF % 2) * 8192 + hc * 512:(tbF % 2) * 8192 + (hc + 1) * 512],
                            start=(hc == 0), stop=(hc == 15)))(hc, bank, wsb, slot, tbF),
                            waits=w, inc=("sMMa", 1) if hc == 15 else None)
                w = [("sMMa", P.val("sMMa") - 1)]
                if icA >= 2:
                    w.append(("sGu", icA - 1))
                P.emit("scalar", (lambda icA, slot: lambda e, s: e.activation(
                    sgv[:, slot * 512:(slot + 1) * 512], pgA[icA % 2][:], AF.Silu))(icA, slot),
                    waits=w, inc=("sSg", 1))
                w = [("sSg", P.val("sSg")), ("sMMa", P.val("sMMa"))]
                if routed == 0 and ic == 0 and tbF >= 1:
                    w.append(("sDm", icg_after_tb))
                P.emit("vector", (lambda icA, ic, slot, gdst: lambda e, s: e.tensor_tensor(
                    gdst[:, ic * 512:(ic + 1) * 512], sgv[:, slot * 512:(slot + 1) * 512],
                    puA[icA % 2][:], OP.mult))(icA, ic, slot, gdst),
                    waits=w)
                P.emit("vector", lambda e, s: e.drain(), inc=("sGu", 1))
                icA += 1
        mma_after_tb.append(P.val("sMMa"))
        sGuW = P.val("sGu")
        for routed in (0, 1):
            nic = (MOE if routed else INTER) // 128
            wsrc = rd_bf if routed else wd_bf
            gsrc = guTr_sb if routed else guT_sb
            for hb in range(4):
                sDe_hb = P.val("sDe")
                for ic in range(nic):
                    slot = icg % 4
                    w = []
                    if icg >= 4:
                        w.append(("sDm", icg - 3))
                    P.emit("sync", (lambda wsrc, ic, hb, slot: lambda e, s: e.dma_start(
                        out=wdt[:, slot * 512:(slot + 1) * 512],
                        in_=wsrc[ic * 128:(ic + 1) * 128, hb * 512:(hb + 1) * 512]))(wsrc, ic, hb, slot),
                        waits=w, inc=(f"dDt{slot}", 16))
                    dv_w = (f"dDt{slot}", P.val(f"dDt{slot}"))
                    for tt4 in range(4):
                        w = []
                        if tt4 == 0:
                            w.append(dv_w)
                        if ic == 0 and tt4 == 0:
                            w.append(("sGu", sGuW))
                            w.append(("sDe", sDe_hb))
                        if ic == 0 and tt4 == 0 and hb == 0 and routed == 0:
                            w.append(("sT2", 8 * tbF))
                        P.emit("tensor", (lambda ic, hb, tt4, slot, gsrc, nic: lambda e, s: e.matmul(
                            pd[tt4][:], gsrc[:, ic * 512 + tt4 * 128: ic * 512 + tt4 * 128 + 128],
                            wdt[:, slot * 512:(slot + 1) * 512],
                            start=(ic == 0), stop=(ic == nic - 1),
                            skip_group_check=True))(ic, hb, tt4, slot, gsrc, nic),
                            waits=w, inc=("sDm", 1) if tt4 == 3 else None)
                    icg += 1
                sDmv = P.val("sDm")
                for tt4 in range(4):
                    if routed == 0:
                        P.emit("scalar", (lambda tt4, hb: lambda e, s: e.copy(
                            dsh[:, tt4 * 2048 + hb * 512: tt4 * 2048 + (hb + 1) * 512],
                            pd[tt4][:]))(tt4, hb),
                            waits=[("sDm", sDmv)] if tt4 == 0 else [], inc=("sDe", 1))
                    else:
                        tg = tbF * 4 + tt4
                        P.emit("vector", (lambda tt4, hb, tg: lambda e, s: e.scalar_tensor_tensor(
                            dsh[:, tt4 * 2048 + hb * 512: tt4 * 2048 + (hb + 1) * 512],
                            pd[tt4][:], comb_sb[:, tg:tg + 1],
                            dsh[:, tt4 * 2048 + hb * 512: tt4 * 2048 + (hb + 1) * 512],
                            OP.mult, OP.add))(tt4, hb, tg),
                            waits=[("sDm", sDmv), ("sDe", P.val("sDe"))] if tt4 == 0
                                  else [("sDe", P.val("sDe"))])
                        P.emit("vector", lambda e, s: e.drain(), inc=("sDe", 1))
        icg_after_tb = P.val("sDm")
        for tt4 in range(4):
            tg = tbF * 4 + tt4
            P.emit("sync", (lambda tg: lambda e, s: e.dma_start(
                out=y1rb[:], in_=y1_scr[tg * 128:(tg + 1) * 128, :]))(tg),
                waits=[("sT2", 2 * tg), ("dY0", 8 * 16), ("dY1", 8 * 16)], inc=("dYr", 16))
            y2v = y2[:, (tg % 2) * 2048:(tg % 2 + 1) * 2048]
            w = [("dYr", P.val("dYr")), ("sDe", P.val("sDe"))]
            if tg >= 2:
                w.append((f"dOut{tg % 2}", 16 * (tg // 2)))
            P.emit("vector", (lambda tt4, y2v: lambda e, s: e.tensor_tensor(
                y2v, dsh[:, tt4 * 2048:(tt4 + 1) * 2048], y1rb[:], OP.add))(tt4, y2v),
                waits=w)
            P.emit("vector", lambda e, s: e.drain(), inc=("sT2", 1))
            P.emit("vector", (lambda y2v: lambda e, s: e.reduce_sum(
                ln2s[:, 0:1], y2v, axis=AX.X, negate=True))(y2v))
            P.emit("vector", lambda e, s: e.drain())
            P.emit("vector", lambda e, s: e.tensor_scalar(ln2s[:, 0:1], ln2s[:, 0:1],
                                                          1.0 / 2048.0, None, OP.mult))
            P.emit("vector", lambda e, s: e.drain(), inc=("sT2", 1))
            P.emit("scalar", (lambda y2v, tg: lambda e, s: e.activation(
                junk3[:], y2v, AF.Square, bias=ln2s[:, 0:1], scale=1.0,
                accum_out=ln2s[:, 1:2]))(y2v, tg),
                waits=[("sT2", 2 * tg + 2)], inc=("sSq", 1))
            P.emit("vector", lambda e, s: e.tensor_scalar(ln2s[:, 2:3], ln2s[:, 1:2],
                                                          1.0 / 2048.0, EPS, OP.mult, OP.add),
                   waits=[("sSq", P.val("sSq"))])
            P.emit("vector", lambda e, s: e.drain(), inc=("sV2", 1))
            P.emit("scalar", lambda e, s: e.activation(ln2s[:, 3:4], ln2s[:, 2:3], AF.Ln),
                   waits=[("sV2", P.val("sV2"))])
            P.emit("scalar", lambda e, s: e.drain())
            P.emit("scalar", lambda e, s: e.activation(ln2s[:, 3:4], ln2s[:, 3:4], AF.Exp,
                                                       scale=-0.5))
            P.emit("scalar", lambda e, s: e.drain(), inc=("sRstd", 1))
            P.emit("vector", (lambda y2v: lambda e, s: e.tensor_scalar(
                y2v, y2v, ln2s[:, 0:1], ln2s[:, 3:4], OP.add, OP.mult))(y2v),
                waits=[("sRstd", P.val("sRstd"))])
            P.emit("vector", lambda e, s: e.drain(), inc=("sY2e", 1))
            P.emit("sync", (lambda tg, y2v: lambda e, s: e.dma_start(
                out=out[tg * 128:(tg + 1) * 128, :], in_=y2v))(tg, y2v),
                waits=[("sY2e", tg + 1)], inc=(f"dOut{tg % 2}", 16))
    P.emit("sync", lambda e, s: e.nop(), waits=[("dOut0", 8 * 16), ("dOut1", 8 * 16),
                                                ("dYr", P.val("dYr")),
                                                ("dYS0", P.val("dYS0")),
                                                ("dYS1", P.val("dYS1"))])
    P.check_deadlock("block")
    with nc.Block() as blk:
        P.replay(blk, sems)
    ps4.close()
    s2b.close()
    g.close()
    n_instr = {e: "closed" for e in ENGS}
    return nc


_NC_CACHE = None


def _get_nc():
    global _NC_CACHE
    if _NC_CACHE is None:
        _NC_CACHE = build_nc()
    return _NC_CACHE


def _make_masks():
    m = np.zeros((128, 2048), np.float32)
    for kt in range(4):
        for kk in range(128):
            m[kk, kt * 512: kt * 512 + kt * 128 + kk] = NEG
    return m


def kernel(L0_attn_x, L0_attn_w_attn, L0_attn_b_attn, L0_attn_w_proj, L0_attn_b_proj,
           L0_ffn_router_weight, L0_ffn_sh_gate, L0_ffn_sh_up, L0_ffn_sh_down,
           L0_ffn_r_gate, L0_ffn_r_up, L0_ffn_r_down):
    x = np.asarray(L0_attn_x, np.float32)
    w_attn = np.ascontiguousarray(np.asarray(L0_attn_w_attn, np.float32))
    b_attn = np.asarray(L0_attn_b_attn, np.float32)
    w_proj = np.ascontiguousarray(np.asarray(L0_attn_w_proj, np.float32))
    b_proj = np.asarray(L0_attn_b_proj, np.float32)

    xT = [np.ascontiguousarray(x[b].T) for b in range(B)]
    wgT = np.ascontiguousarray(np.asarray(L0_ffn_sh_gate, np.float32).T)
    wuT = np.ascontiguousarray(np.asarray(L0_ffn_sh_up, np.float32).T)
    wdT = np.ascontiguousarray(np.asarray(L0_ffn_sh_down, np.float32).T)
    rgT = np.ascontiguousarray(np.asarray(L0_ffn_r_gate, np.float32).T)
    ruT = np.ascontiguousarray(np.asarray(L0_ffn_r_up, np.float32).T)
    rdT = np.ascontiguousarray(np.asarray(L0_ffn_r_down, np.float32).T)
    rwT = np.ascontiguousarray(np.asarray(L0_ffn_router_weight, np.float32).T)
    bpr = np.ascontiguousarray(b_proj[None, :])
    masks = _make_masks()

    in_maps = []
    for k in range(8):
        b, hg = divmod(k, 4)
        f0 = hg * FPC
        wq = w_attn[:, f0:f0 + FPC]
        wk = w_attn[:, C + f0:C + f0 + FPC]
        wv = w_attn[:, 2 * C + f0:2 * C + f0 + FPC]
        wqkv_np = np.ascontiguousarray(np.concatenate([wq, wk, wv], axis=1))
        bq = b_attn[f0:f0 + FPC].reshape(4, 128).T
        bk = b_attn[C + f0:C + f0 + FPC].reshape(4, 128).T
        bqk_np = np.ascontiguousarray(np.concatenate([bq, bk], axis=1))
        bv = np.ascontiguousarray(b_attn[2 * C + f0:2 * C + f0 + FPC][None, :])
        tok0 = hg * NTOK
        in_maps.append({
            "xT": xT[b], "x_own": np.ascontiguousarray(x[b, tok0:tok0 + NTOK, :]),
            "wqkv": wqkv_np, "bqk": bqk_np, "bv_row": bv,
            "wproj": w_proj, "bp_row": bpr, "rwT": rwT,
            "wgT": wgT, "wuT": wuT, "wdT": wdT,
            "rgT": rgT, "ruT": ruT, "rdT": rdT,
            "masks": masks,
        })
    nc = _get_nc()
    res = run_bass_kernel_spmd(nc, in_maps, core_ids=list(range(8)))
    full = np.empty((B, T, C), np.float32)
    for k in range(8):
        b, hg = divmod(k, 4)
        tok0 = hg * NTOK
        full[b, tok0:tok0 + NTOK, :] = res.results[k]["out"]
    return full


if __name__ == "__main__":
    build_nc()
    print("built ok")
